# version 06
# baseline (speedup 1.0000x reference)
import math
import os
from contextlib import ExitStack

import numpy as np
import ml_dtypes
import concourse.bass as bass
import concourse.mybir as mybir
from concourse.bass_utils import run_bass_kernel_spmd

F32 = mybir.dt.float32
BF16 = mybir.dt.bfloat16
U32 = mybir.dt.uint32
AF = mybir.ActivationFunctionType
ALU = mybir.AluOpType
AX = mybir.AxisListType

B, T, D = 4, 2048, 2048
DT = D // 128
TK = 1024
ST1 = TK + 1
H, HD = 16, 128
E, FF = 8, 512
EPS = 1e-5
LN1E8 = 18.420680743952367
SCALE = 1.0 / math.sqrt(HD)
NE = 512
WCH = 256

ENGS = ("sync", "gpsimd", "tensor", "vector", "scalar")


class Rec:
    __slots__ = ("op", "args", "kwargs", "incs")

    def __init__(self, op, args, kwargs):
        self.op, self.args, self.kwargs = op, args, kwargs
        self.incs = []

    def then_inc(self, sem, n):
        for i, (s0, n0) in enumerate(self.incs):
            if s0 is sem:
                self.incs[i] = (s0, n0 + n)
                return self
        self.incs.append((sem, n))
        return self


class Sem:
    def __init__(self, handle, name):
        self.h = handle
        self.name = name
        self.n = 0


AUTO_OPS = {
    "tensor_copy", "tensor_tensor", "tensor_add", "tensor_sub", "tensor_mul",
    "tensor_max", "tensor_scalar", "tensor_single_scalar", "tensor_scalar_add",
    "tensor_scalar_sub", "tensor_scalar_mul", "tensor_scalar_max",
    "tensor_scalar_min", "tensor_relu", "tensor_reduce", "reduce_max",
    "reduce_sum", "scalar_tensor_tensor", "reciprocal", "memset", "select",
    "activation", "copy", "add", "mul",
}


class Stream:
    def __init__(self, name):
        self.name = name
        self.q = []
        self.wait_cache = {}
        self.auto_sem = None
        self.auto_n = 0

    def emit(self, op, *args, **kwargs):
        if self.auto_sem is not None and op in AUTO_OPS:
            if self.auto_sem.n > 0:
                self.q.append(Rec("wait_ge", (self.auto_sem.h, self.auto_sem.n),
                                  {}))
            r = Rec(op, args, kwargs)
            self.q.append(r)
            r.then_inc(self.auto_sem, 1)
            self.auto_sem.n += 1
            return r
        r = Rec(op, args, kwargs)
        self.q.append(r)
        return r

    def __getattr__(self, op):
        if op.startswith("_"):
            raise AttributeError(op)
        return lambda *a, **k: self.emit(op, *a, **k)

    def wait(self, sem, n=None):
        if n is None:
            n = sem.n
        if n <= 0:
            return
        if self.wait_cache.get(id(sem), -1) >= n:
            return
        self.wait_cache[id(sem)] = n
        self.emit("wait_ge", sem.h, n)

    def mark(self, sem, k=1):
        self.q[-1].then_inc(sem, k)
        sem.n += k
        return sem.n

    def while_begin(self, reg):
        self.q.append(("WHILE_BEGIN", reg))

    def while_end(self):
        self.q.append(("WHILE_END",))


def chunks_of(total, size=NE):
    out, p = [], 0
    while p < total:
        out.append((p, min(size, total - p)))
        p += size
    return out


def build_program(upto="full"):
    nc = bass.Bass("TRN2", target_bir_lowering=False)
    ctx = ExitStack()
    S, G, TN, V, SC = (Stream(n) for n in ENGS)

    sems = {}
    for name in ("LDA", "LDW", "TEN", "VEC", "SCA", "STO", "POL", "GPD",
                 "LW0", "LW1", "LDS", "XS0", "XS1", "XS2", "XS3"):
        sems[name] = Sem(ctx.enter_context(nc.semaphore(name)), name)
    LDA, LDW, TEN, VEC, SCA, STO, POL, GPD = (sems[k] for k in
        ("LDA", "LDW", "TEN", "VEC", "SCA", "STO", "POL", "GPD"))
    LW = (sems["LW0"], sems["LW1"])
    XSEM = (sems["XS0"], sems["XS1"], sems["XS2"], sems["XS3"])

    V.auto_sem = None
    SC.auto_sem = None

    def din(name, shape, dtype=F32):
        return nc.dram_tensor(name, list(shape), dtype, kind="ExternalInput")

    x0T_d = din("x0T", [D, TK])
    wT_d = {k: din(k, [D, D]) for k in
            ("wrT", "wkT", "wvT", "woT", "cwrT", "cwkT", "cwvT", "oT")}
    qTh_d = din("qTh", [D, 8 * HD])
    kTh_d = din("kTh", [D, 8 * HD])
    vTh_d = din("vTh", [D, 8 * HD])
    gT_d = din("gT", [HD, HD])
    rT_d = din("rT", [128, DT * E])
    w1_d = din("w1", [E, D, FF])
    w3_d = din("w3", [E, D, FF])
    w2_d = din("w2", [E, FF, D])
    vec_names = ("ln1g", "ln1b", "ln2g", "ln2b", "mur", "muk", "muv",
                 "cmur", "cmuk", "ubias")
    vec_d = {k: din(k, [128, DT]) for k in vec_names}
    gb_d = din("gb", [HD, 1])
    maskh_d = din("maskh", [128, 1])
    idf_d = din("idf", [128, 128], BF16)
    idf32_d = din("idf32", [128, 128])
    sel8_d = din("sel8", [E, E * 128], BF16)
    tri_d = din("tri", [128, 128], BF16)
    flagv_d = din("flagv", [1, 8], U32)
    hinfo_d = din("hinfo", [1, 4], U32)
    yout_d = nc.dram_tensor("yout", [D, TK], F32, kind="ExternalOutput")

    colsh_d = nc.dram_tensor("colsh", [2, 128, DT], BF16, kind="Internal",
                             addr_space="Shared")
    colxm_d = nc.dram_tensor("colxm", [2, 128, DT], F32, kind="Internal",
                             addr_space="Shared")
    flsh_d = nc.dram_tensor("flsh", [1, 16], U32, kind="Internal",
                            addr_space="Shared")
    if upto not in ("s1",):
        x2sh_d = nc.dram_tensor("x2sh", [2, D, TK], F32, kind="Internal",
                                addr_space="Shared")
        avsh_d = nc.dram_tensor("avsh", [2, 2 * 8 * 128, TK], BF16,
                                kind="Internal", addr_space="Shared")

    def sb(name, shape, dtype):
        return ctx.enter_context(nc.sbuf_tensor(name, list(shape), dtype))

    vec_s = {k: sb("s_" + k, [128, DT], F32) for k in vec_names}
    gb_s = sb("s_gb", [HD, 1], F32)
    maskh_s = sb("s_maskh", [128, 1], F32)
    idf_s = sb("s_idf", [128, 128], BF16)
    idf32_s = sb("s_idf32", [128, 128], F32)
    rTf_s = sb("s_rTf", [128, DT * E], F32)
    onesf_s = sb("s_onesf", [128, 1], F32)
    col_s = sb("s_col", [128, 1], F32)
    sel8_s = sb("s_sel8", [E, E * 128], BF16)
    tri_s = sb("s_tri", [128, 128], BF16)
    gT_s = sb("s_gT", [HD, HD], BF16)
    flagv_s = sb("s_flagv", [1, 8], U32)
    hinfo_s = sb("s_hinfo", [1, 4], U32)
    sflag_s = sb("s_sflag", [1, 1], U32)
    onesb_s = sb("s_onesb", [128, 1], BF16)
    ones1f_s = sb("s_ones1f", [1, 128], F32)
    ubias2_s = sb("s_ubias2", [128, DT], F32)
    omu_s = {k: sb("s_omu_" + k, [128, DT], F32)
             for k in ("mur", "muk", "muv", "cmur", "cmuk")}
    colbuf_s = sb("s_colbuf", [128, DT], BF16)
    colbuf2_s = sb("s_colbuf2", [128, DT], F32)
    rowA = sb("s_rowA", [1, ST1 + 31], F32)
    rowB = sb("s_rowB", [1, ST1 + 31], F32)
    rowC = sb("s_rowC", [1, ST1 + 31], F32)
    hs_s = sb("s_hs", [1, 64], F32)
    mh_s = sb("s_mh", [128, 8], F32)
    sc8_s = sb("s_sc8", [128, 64], F32)
    sc8b_s = sb("s_sc8b", [128, 64], F32)
    wfin_s = sb("s_wfin", [128, 64], BF16)
    top2_s = sb("s_top2", [128, 64], F32)
    top2b_s = sb("s_top2b", [128, 64], F32)
    wTs_s = sb("s_wTs", [8, TK], BF16)
    wb_s = sb("s_wb", [128, TK], BF16)
    rb_s = sb("s_rb", [128, NE], F32)

    RES = sb("RES", [128, DT * ST1], F32)
    HB = sb("HB", [128, DT * ST1], BF16)
    XS = sb("XS", [128, 2 * DT * TK], BF16)
    WB = [sb(f"WB{i}", [128, DT * WCH], BF16) for i in range(2)]
    SCR = sb("SCR", [128, 2112], BF16)

    banks = [ctx.enter_context(nc.psum_tensor(f"pb{i}", [128, NE], F32))
             for i in range(7)]
    pT8 = ctx.enter_context(nc.psum_tensor("pT8", [8, 128], BF16))

    bank_state = [(VEC, 0)] * 7
    bank_rr = [0]

    bank_hold = set()

    def take_bank():
        while True:
            i = bank_rr[0] % 7
            bank_rr[0] += 1
            if i not in bank_hold:
                break
        sem, cnt = bank_state[i]
        TN.wait(sem, cnt)
        return i

    def bank_done(i, sem, cnt=None):
        bank_state[i] = (sem, cnt if cnt is not None else sem.n)

    rg = {k: nc.gpsimd.alloc_register(k) for k in
          ("roff", "roff2", "rexp", "rflag", "rcond", "rtgt", "rmyhalf",
           "rsibhalf")}

    def RES3():
        return RES[:].rearrange("p (dt l) -> p dt l", dt=DT)

    def HB3():
        return HB[:].rearrange("p (dt l) -> p dt l", dt=DT)

    def res_t(dt, c0, L):
        return RES[:, dt * ST1 + c0: dt * ST1 + c0 + L]

    def hb_t(dt, c0, L):
        return HB[:, dt * ST1 + c0: dt * ST1 + c0 + L]

    def xs_b(c0, L):
        return XS[:, c0:c0 + L]

    def xs_f(c0f, L):
        return XS[:].bitcast(F32)[:, c0f:c0f + L]

    def scr_b(c0, L):
        return SCR[:, c0:c0 + L]

    def scr_f(c0f, L):
        return SCR[:].bitcast(F32)[:, c0f:c0f + L]

    SQ_SLOT = (0, 1056)
    PR_SLOT = (0, 512, 1024, 1536)
    TS32 = 8192

    def sdma(out, in_):
        S.dma_start(out=out, in_=in_)
        S.mark(LDA, 16)
        return LDA.n

    for k in vec_names:
        sdma(vec_s[k][:], vec_d[k].ap())
    sdma(gb_s[:], gb_d.ap())
    sdma(maskh_s[:], maskh_d.ap())
    sdma(idf_s[:], idf_d.ap())
    sdma(idf32_s[:], idf32_d.ap())
    sdma(rTf_s[:], rT_d.ap())
    sdma(sel8_s[:], sel8_d.ap())
    sdma(flagv_s[:], flagv_d.ap())
    sdma(hinfo_s[:], hinfo_d.ap())
    sdma(XS[:].bitcast(F32).rearrange("p (dt l) -> p dt l", dt=DT),
         x0T_d.ap().rearrange("(dt p) l -> p dt l", p=128))
    ld_pre = LDA.n
    G.dma_start(out=gT_s[:], in_=gT_d.ap())
    G.mark(sems["LDS"], 16)
    G.dma_start(out=tri_s[:], in_=tri_d.ap())
    G.mark(sems["LDS"], 16)
    ldw_pre = (sems["LDS"], sems["LDS"].n)

    V.memset(onesb_s[:], 1.0)
    V.mark(VEC)
    V.memset(ones1f_s[:], 1.0)
    V.mark(VEC)
    V.memset(onesf_s[:], 1.0)
    V.mark(VEC)
    V.wait(LDA, ld_pre)
    V.tensor_scalar_add(ubias2_s[:], vec_s["ubias"][:], LN1E8)
    V.mark(VEC)
    for k in ("mur", "muk", "muv", "cmur", "cmuk"):
        V.tensor_scalar(omu_s[k][:], vec_s[k][:], -1.0, 1.0, ALU.mult, ALU.add)
        V.mark(VEC)
    V.wait(VEC, VEC.n)

    G.wait(LDA, ld_pre)
    G.reg_load(rg["rmyhalf"], hinfo_s[0:1, 0:1])
    G.reg_load(rg["rsibhalf"], hinfo_s[0:1, 1:2])
    G.reg_mov(rg["rtgt"], 0)

    fl_phase = [0]

    def pair_signal(payload_sem_target):
        ph = fl_phase[0]
        G.wait(GPD, payload_sem_target)
        G.reg_alu(rg["roff"], rg["rmyhalf"], ph * 2, ALU.add)
        G.dma_start(out=bass.AP(flsh_d, rg["roff"], [[1, 1], [1, 1]]),
                    in_=flagv_s[0:1, ph:ph + 1])
        G.mark(GPD, 16)

    def pair_poll():
        ph = fl_phase[0]
        G.wait(GPD, GPD.n)
        G.reg_alu(rg["roff"], rg["rsibhalf"], ph * 2, ALU.add)
        G.reg_load(rg["rexp"], flagv_s[0:1, ph:ph + 1])
        G.reg_mov(rg["rcond"], 1)
        G.while_begin(rg["rcond"])
        G.dma_start(out=sflag_s[:],
                    in_=bass.AP(flsh_d, rg["roff"], [[1, 1], [1, 1]]))
        G.q[-1].then_inc(POL, 16)
        G.reg_alu(rg["rtgt"], rg["rtgt"], 16, ALU.add)
        G.emit("wait_ge", POL.h, rg["rtgt"])
        G.reg_load(rg["rflag"], sflag_s[0:1, 0:1])
        G.reg_alu(rg["rcond"], rg["rflag"], rg["rexp"], ALU.not_equal)
        G.while_end()
        fl_phase[0] += 1

    wb_state = [(TEN, 0), (TEN, 0)]
    wb_rr = [0]

    def wload(dram_h, col0, cw, rows=D, expert=None):
        i = wb_rr[0] % 2
        wb_rr[0] += 1
        sem, cnt = wb_state[i]
        G.wait(sem, cnt)
        rt = rows // 128
        src = dram_h.ap()
        if expert is not None:
            src = src[expert]
        src = src[:, col0:col0 + cw].rearrange("(rt p) e -> p rt e", p=128)
        dst = WB[i][:, 0:rt * cw].rearrange("p (rt e) -> p rt e", rt=rt)
        G.dma_start(out=dst, in_=src)
        G.mark(LW[i], 16)
        return i, (LW[i], LW[i].n)

    def wtile(i, ct, e0, ew, cw=WCH):
        return WB[i][:, ct * cw + e0: ct * cw + e0 + ew]

    def wb_used(i):
        wb_state[i] = (TEN, TEN.n)

    def emit_rows(b1, b2, c0, Lc, stat_done):
        rA, rB, rC = rowA[0:1, c0:c0 + Lc], rowB[0:1, c0:c0 + Lc], \
            rowC[0:1, c0:c0 + Lc]
        V.wait(TEN, stat_done)
        V.tensor_scalar_mul(rC, banks[b1][0:1, 0:Lc], 1.0 / D)
        mC = V.mark(VEC)
        V.tensor_scalar(rA, banks[b2][0:1, 0:Lc], 1.0 / D, EPS,
                        ALU.mult, ALU.add)
        V.mark(VEC)
        V.wait(VEC, VEC.n)
        V.tensor_mul(rB, rC, rC)
        V.mark(VEC)
        V.wait(VEC, VEC.n)
        V.tensor_sub(rA, rA, rB)
        mV = V.mark(VEC)
        bank_done(b1, VEC, mC)
        bank_done(b2, VEC, mV)
        SC.wait(VEC, mV)
        SC.emit("activation", rB, rA, AF.Ln)
        SC.mark(SCA)
        SC.wait(SCA, SCA.n)
        SC.emit("activation", rA, rB, AF.Exp, scale=-0.5)
        mS = SC.mark(SCA)
        V.wait(SCA, mS)
        V.tensor_mul(rB, rC, rA)
        V.mark(VEC)
        V.wait(VEC, VEC.n)
        V.tensor_scalar_mul(rB, rB, -1.0)
        mB = V.mark(VEC)
        V.wait(VEC, mB)
        bA = take_bank()
        TN.wait(SCA, mS)
        TN.matmul(banks[bA][:, 0:Lc], ones1f_s[:], rA, start=True, stop=True)
        bB = take_bank()
        TN.wait(VEC, mB)
        TN.matmul(banks[bB][:, 0:Lc], ones1f_s[:], rB, start=True, stop=True)
        mT = TN.mark(TEN)
        return (c0, Lc, bA, bB, mT)

    def emit_ln(src_fn, L, cast_fn, apply_fn, debug_tap=None):
        ch = chunks_of(L)
        stat_banks = [(take_bank(), take_bank()) for _ in ch]
        cast_marks, sq_marks, mm_marks = [], [], []
        for dt in range(DT):
            V.tensor_copy(cast_fn(dt), src_fn(dt))
            cast_marks.append(V.mark(VEC))
            if dt >= 2:
                SC.wait(TEN, mm_marks[dt - 2])
            SC.wait(VEC, cast_marks[dt])
            SC.emit("activation", scr_b(SQ_SLOT[dt % 2], L), cast_fn(dt),
                    AF.Square)
            sq_marks.append(SC.mark(SCA))
            TN.wait(VEC, cast_marks[dt])
            TN.wait(SCA, sq_marks[dt])
            for ci, (c0, Lc) in enumerate(ch):
                b1, b2 = stat_banks[ci]
                TN.matmul(banks[b1][0:1, 0:Lc], onesb_s[:],
                          cast_fn(dt)[:, c0:c0 + Lc],
                          start=(dt == 0), stop=(dt == DT - 1))
                TN.matmul(banks[b2][0:1, 0:Lc], onesb_s[:],
                          scr_b(SQ_SLOT[dt % 2] + c0, Lc),
                          start=(dt == 0), stop=(dt == DT - 1))
            mm_marks.append(TN.mark(TEN))
        stat_done = TEN.n
        if debug_tap is not None:
            b1d, b2d = stat_banks[debug_tap]
            c0d, Ld = ch[debug_tap]
            V.wait(TEN, stat_done)
            V.tensor_copy(hs_s[0:1, 0:2], banks[b1d][0:1, 0:2])
            V.mark(VEC)
            V.tensor_copy(hs_s[0:1, 2:4], banks[b2d][0:1, 0:2])
            V.mark(VEC)
        out_chunks = []
        for ci, (c0, Lc) in enumerate(ch):
            b1, b2 = stat_banks[ci]
            out_chunks.append(emit_rows(b1, b2, c0, Lc, stat_done))
        for (c0, Lc, bA, bB, mT) in out_chunks:
            V.wait(TEN, mT)
            for dt in range(DT):
                apply_fn(dt, c0, Lc, banks[bA], banks[bB])
            mdone = V.mark(VEC)
            bank_done(bA, VEC, mdone)
            bank_done(bB, VEC, mdone)

    def emit_output(col0, stride):
        S.wait(VEC, VEC.n)
        S.wait(SCA, SCA.n)
        if stride == ST1:
            src = RES3()[:, :, col0:col0 + TK]
        else:
            src = RES[:, 0:DT * TK].rearrange("p (dt l) -> p dt l", dt=DT)
        S.dma_start(out=yout_d.ap().rearrange("(dt p) l -> p dt l", p=128),
                    in_=src)
        S.mark(STO, 16)
        S.wait(STO, STO.n)
        G.wait(STO, STO.n)



    def xn_t(dt):
        return HB[:, dt * TK: dt * TK + TK]

    def emit_moe(res_fn):
        def moe_apply(dt, c0, Lc, bkA, bkB):
            t = xs_f(TS32, Lc)
            V.tensor_mul(t, res_fn(dt)[:, c0:c0 + Lc], bkA[:, 0:Lc])
            vsync_small(Lc)
            V.tensor_add(xn_t(dt)[:, c0:c0 + Lc], t, bkB[:, 0:Lc])
            vsync_small(Lc)

        emit_ln(res_fn, TK, xn_t, moe_apply)
        xn_done = V.mark(VEC)

        TN.wait(LDA, ld_pre)
        bkr = take_bank()
        for ct in range(DT):
            TN.matmul(banks[bkr][0:1, 0:E], onesf_s[:],
                      rTf_s[:, ct * E:(ct + 1) * E],
                      start=(ct == 0), stop=(ct == DT - 1))
        tm = TN.mark(TEN)
        V.wait(TEN, tm)
        V.tensor_copy(hs_s[0:1, 8:16], banks[bkr][0:1, 0:E])
        m_rsum = V.mark(VEC)
        bank_done(bkr, VEC, m_rsum)
        for tc in range(2):
            bkr = take_bank()
            for ct in range(DT):
                TN.matmul(banks[bkr][0:1, 0:NE], onesf_s[:],
                          res_fn(ct)[:, tc * NE:(tc + 1) * NE],
                          start=(ct == 0), stop=(ct == DT - 1))
            tm = TN.mark(TEN)
            V.wait(TEN, tm)
            V.tensor_scalar_mul(rowC[0:1, tc * NE:(tc + 1) * NE],
                                banks[bkr][0:1, 0:NE], 1.0 / D)
            m_rsum = V.mark(VEC)
            bank_done(bkr, VEC, m_rsum)
        V.wait(VEC, VEC.n)
        V.tensor_mul(rowB[0:1, 0:TK], rowC[0:1, 0:TK], rowA[0:1, 0:TK])
        V.mark(VEC)
        V.wait(VEC, VEC.n)
        V.tensor_scalar_mul(rowB[0:1, 0:TK], rowB[0:1, 0:TK], -1.0)
        mrB = V.mark(VEC)
        for tt in range(8):
            bkS = take_bank()
            for ct in range(DT):
                TN.matmul(banks[bkS][:, 0:E],
                          res_fn(ct)[:, tt * 128:(tt + 1) * 128],
                          rTf_s[:, ct * E:(ct + 1) * E],
                          start=(ct == 0), stop=(ct == DT - 1))
            tmS = TN.mark(TEN)
            bkO = take_bank()
            TN.wait(VEC, mrB)
            TN.matmul(banks[bkO][:, 0:E], rowB[0:1, tt * 128:(tt + 1) * 128],
                      hs_s[0:1, 8:16], start=True, stop=True)
            tmO = TN.mark(TEN)
            bkT = take_bank()
            TN.transpose(banks[bkT][:, 0:1], rowA[0:1, tt * 128:(tt + 1) * 128],
                         idf32_s[0:1, 0:1])
            tmT = TN.mark(TEN)
            V.wait(TEN, tmT)
            V.tensor_copy(col_s[:], banks[bkT][:, 0:1])
            bank_done(bkT, VEC, V.mark(VEC))
            V.wait(VEC, VEC.n)
            V.tensor_scalar_mul(sc8_s[:, tt * E:(tt + 1) * E],
                                banks[bkS][:, 0:E], col_s[:, 0:1])
            bank_done(bkS, VEC, V.mark(VEC))
            V.wait(VEC, VEC.n)
            V.tensor_add(sc8_s[:, tt * E:(tt + 1) * E],
                         sc8_s[:, tt * E:(tt + 1) * E], banks[bkO][:, 0:E])
            bank_done(bkO, VEC, V.mark(VEC))
            V.wait(VEC, VEC.n)

        def S3(t):
            return t[:].rearrange("p (tt e) -> p tt e", e=E)

        m1 = sc8b_s[:, 0:8]
        negm1 = sc8b_s[:, 8:16]
        m2 = sc8b_s[:, 16:24]
        dtmp = sc8b_s[:, 24:32]
        rcp = sc8b_s[:, 32:40]

        def vs():
            V.mark(VEC)
            V.wait(VEC, VEC.n)

        def ss():
            SC.mark(SCA)
            SC.wait(SCA, SCA.n)

        V.wait(VEC, VEC.n)
        V.reduce_max(m1.rearrange("p (a b) -> p a b", b=1), S3(sc8_s), axis=AX.X)
        vs()
        V.tensor_scalar_mul(negm1, m1, -1.0)
        vs()
        for tt in range(8):
            V.tensor_scalar(top2_s[:, tt * E:(tt + 1) * E],
                            sc8_s[:, tt * E:(tt + 1) * E],
                            m1[:, tt:tt + 1], None, ALU.is_equal)
            vs()
        V.tensor_scalar_mul(top2_s[:], top2_s[:], -2.0e30)
        vs()
        V.tensor_add(top2_s[:], top2_s[:], sc8_s[:])
        vs()
        V.reduce_max(m2.rearrange("p (a b) -> p a b", b=1), S3(top2_s), axis=AX.X)
        vs()
        for tt in range(8):
            SC.wait(VEC, VEC.n)
            SC.emit("activation", top2_s[:, tt * E:(tt + 1) * E],
                    sc8_s[:, tt * E:(tt + 1) * E], AF.Exp,
                    bias=negm1[:, tt:tt + 1])
            ss()
        V.wait(SCA, SCA.n)
        V.tensor_add(dtmp, m2, negm1)
        vs()
        SC.wait(VEC, VEC.n)
        SC.emit("activation", dtmp, dtmp, AF.Exp)
        ss()
        V.wait(SCA, SCA.n)
        V.tensor_scalar_add(dtmp, dtmp, 1.0)
        vs()
        V.reciprocal(rcp, dtmp)
        vs()
        for tt in range(8):
            V.tensor_scalar(top2b_s[:, tt * E:(tt + 1) * E],
                            sc8_s[:, tt * E:(tt + 1) * E],
                            m2[:, tt:tt + 1], None, ALU.is_ge)
            vs()
        V.tensor_mul(top2b_s[:], top2b_s[:], top2_s[:])
        vs()
        for tt in range(8):
            V.tensor_scalar_mul(wfin_s[:, tt * E:(tt + 1) * E],
                                top2b_s[:, tt * E:(tt + 1) * E],
                                rcp[:, tt:tt + 1])
            vs()
        w_done = VEC.n

        if upto == "moescores":
            G.wait(VEC, VEC.n)
            G.wait(SCA, SCA.n)
            G.dma_start(out=yout_d.ap()[0:128, 0:64], in_=sc8_s[:])
            G.mark(STO, 16)
            G.dma_start(out=yout_d.ap()[128:256, 0:64], in_=wfin_s[:])
            G.mark(STO, 16)
            G.dma_start(out=yout_d.ap()[256:384, 0:64], in_=top2_s[:])
            G.mark(STO, 16)
            G.dma_start(out=yout_d.ap()[384:512, 0:8], in_=m1)
            G.mark(STO, 16)
            G.dma_start(out=yout_d.ap()[512:640, 0:8], in_=m2)
            G.mark(STO, 16)
            return "STOP"

        for tt in range(8):
            TN.wait(VEC, w_done if tt == 0 else VEC.n)
            TN.transpose(pT8[0:8, 0:128], wfin_s[:, tt * E:(tt + 1) * E],
                         idf_s[:])
            tm = TN.mark(TEN)
            V.wait(TEN, tm)
            V.tensor_copy(wTs_s[0:8, tt * 128:(tt + 1) * 128], pT8[0:8, 0:128])
            V.mark(VEC)
        wts_done = VEC.n

        silu_marks = {}
        for e in range(E):
            for tc in range(2):
                bk = take_bank()
                TN.wait(VEC, wts_done)
                TN.matmul(banks[bk][:, 0:NE], sel8_s[:, e * 128:(e + 1) * 128],
                          wTs_s[0:8, tc * NE:(tc + 1) * NE],
                          start=True, stop=True)
                tm = TN.mark(TEN)
                V.wait(TEN, tm)
                V.tensor_copy(wb_s[:, tc * NE:(tc + 1) * NE], banks[bk][:, 0:NE])
                bank_done(bk, VEC, V.mark(VEC))
            wb_done = VEC.n
            for wc in range(2):
                wi, ldwn = wload(w1_d, wc * 256, 256, rows=D, expert=e)
                TN.wait(*ldwn)
                for fo in range(2):
                    ft = wc * 2 + fo
                    for ts in range(2):
                        bk = take_bank()
                        for ct in range(DT):
                            TN.matmul(banks[bk][:, 0:NE],
                                      WB[wi][:, ct * 256 + fo * 128:
                                             ct * 256 + fo * 128 + 128],
                                      xn_t(ct)[:, ts * NE:(ts + 1) * NE],
                                      start=(ct == 0), stop=(ct == DT - 1))
                        tm = TN.mark(TEN)
                        slot = e * 4096 + ft * 1024 + ts * NE
                        SC.wait(TEN, tm)
                        SC.emit("activation", xs_b(slot, NE), banks[bk][:, 0:NE],
                                AF.Silu)
                        ms = SC.mark(SCA)
                        silu_marks[(e, ft, ts)] = ms
                        bank_done(bk, SCA, ms)
                wb_used(wi)
            for wc in range(2):
                wi, ldwn = wload(w3_d, wc * 256, 256, rows=D, expert=e)
                TN.wait(*ldwn)
                for fo in range(2):
                    ft = wc * 2 + fo
                    for ts in range(2):
                        bk = take_bank()
                        for ct in range(DT):
                            TN.matmul(banks[bk][:, 0:NE],
                                      WB[wi][:, ct * 256 + fo * 128:
                                             ct * 256 + fo * 128 + 128],
                                      xn_t(ct)[:, ts * NE:(ts + 1) * NE],
                                      start=(ct == 0), stop=(ct == DT - 1))
                        tm = TN.mark(TEN)
                        slot = e * 4096 + ft * 1024 + ts * NE
                        V.wait(TEN, tm)
                        V.wait(SCA, silu_marks[(e, ft, ts)])
                        V.wait(VEC, wb_done)
                        V.tensor_mul(xs_b(slot, NE), banks[bk][:, 0:NE],
                                     xs_b(slot, NE))
                        bank_done(bk, VEC, V.mark(VEC))
                        V.tensor_mul(xs_b(slot, NE), xs_b(slot, NE),
                                     wb_s[:, ts * NE:(ts + 1) * NE])
                        V.mark(VEC)
                wb_used(wi)
        h_done = VEC.n

        for dg in range(8):
            acc = {}
            for e in range(E):
                wi, ldwn = wload(w2_d, dg * 256, 256, rows=FF, expert=e)
                TN.wait(*ldwn)
                TN.wait(VEC, h_done)
                for do_ in range(2):
                    for ts in range(2):
                        if e == 0:
                            acc[(do_, ts)] = take_bank()
                        bk = acc[(do_, ts)]
                        for ftile in range(4):
                            TN.matmul(banks[bk][:, 0:NE],
                                      WB[wi][:, ftile * 256 + do_ * 128:
                                             ftile * 256 + do_ * 128 + 128],
                                      xs_b(e * 4096 + ftile * 1024 + ts * NE,
                                           NE),
                                      start=(e == 0 and ftile == 0),
                                      stop=(e == E - 1 and ftile == 3))
                        if e == E - 1:
                            tm = TN.mark(TEN)
                            dt = dg * 2 + do_
                            V.wait(TEN, tm)
                            V.tensor_add(res_fn(dt)[:, ts * NE:(ts + 1) * NE],
                                         res_fn(dt)[:, ts * NE:(ts + 1) * NE],
                                         banks[bk][:, 0:NE])
                            bank_done(bk, VEC, V.mark(VEC))
                TN.mark(TEN)
                wb_used(wi)


    QB = 0
    KB = DT * TK
    AVB = DT * TK

    def RESb(col, L):
        return RES[:].bitcast(BF16)[:, col:col + L]

    def emit_attention():
        G.wait(VEC, VEC.n)
        G.wait(SCA, SCA.n)
        G.reg_alu(rg["roff"], rg["rmyhalf"], D * TK, ALU.mult)
        G.dma_start(out=bass.AP(x2sh_d, rg["roff"],
                                [[TK, 128], [128 * TK, DT], [1, TK]]),
                    in_=RES3()[:, :, 1:1 + TK])
        G.mark(GPD, 16)
        pair_signal(GPD.n)
        pair_poll()

        slot_fence = {0: (TEN, 0), 1: (TEN, 0), 2: (TEN, 0), 3: (TEN, 0)}

        def ln3_xn(th):
            if th == 0:
                G.wait(TEN, TEN.n)
                G.wait(VEC, VEC.n)
                G.wait(SCA, SCA.n)
            ch = ((0, NE), (NE, NE))
            stat_banks = [(take_bank(), take_bank()) for _ in ch]
            for bk2 in stat_banks:
                bank_hold.add(bk2[0]); bank_hold.add(bk2[1])
            sqf = {0: (SCA, 0), 1: (SCA, 0)}
            for dt in range(DT):
                for tc in range(2):
                    sl = (dt * 2 + tc) % 2
                    sem, cnt = slot_fence[sl]
                    G.wait(sem, cnt)
                    G.dma_start(out=scr_b(PR_SLOT[sl], NE),
                                in_=x2sh_d.ap()[th, dt * 128:(dt + 1) * 128,
                                                tc * NE:(tc + 1) * NE])
                    G.mark(XSEM[sl], 16)
                    gm = (XSEM[sl], XSEM[sl].n)
                    sem, cnt = sqf[sl]
                    SC.wait(sem, cnt)
                    SC.wait(*gm)
                    SC.emit("activation", scr_b(PR_SLOT[2 + sl], NE),
                            scr_b(PR_SLOT[sl], NE), AF.Square)
                    ms = SC.mark(SCA)
                    b1, b2 = stat_banks[tc]
                    TN.wait(*gm)
                    TN.wait(SCA, ms)
                    TN.matmul(banks[b1][0:1, 0:NE], onesb_s[:],
                              scr_b(PR_SLOT[sl], NE),
                              start=(dt == 0), stop=(dt == DT - 1))
                    TN.matmul(banks[b2][0:1, 0:NE], onesb_s[:],
                              scr_b(PR_SLOT[2 + sl], NE),
                              start=(dt == 0), stop=(dt == DT - 1))
                    tm = TN.mark(TEN)
                    slot_fence[sl] = (TEN, tm)
                    slot_fence[2 + sl] = (TEN, tm)
                    sqf[sl] = (TEN, tm)
            stat_done = TEN.n
            for bk2 in stat_banks:
                bank_hold.discard(bk2[0]); bank_hold.discard(bk2[1])
            out_chunks = [emit_rows(b1, b2, c0, Lc, stat_done)
                          for (b1, b2), (c0, Lc) in zip(stat_banks, ch)]
            for (c0, Lc, bA, bB, mT) in out_chunks:
                tc = 0 if c0 == 0 else 1
                for dt in range(DT):
                    sl = (dt * 2 + tc) % 4
                    sem, cnt = slot_fence[sl]
                    G.wait(sem, cnt)
                    G.dma_start(out=scr_b(PR_SLOT[sl], NE),
                                in_=x2sh_d.ap()[th, dt * 128:(dt + 1) * 128,
                                                c0:c0 + Lc])
                    G.mark(XSEM[sl], 16)
                    V.wait(XSEM[sl], XSEM[sl].n)
                    V.wait(TEN, mT)
                    t = xs_f(TS32, Lc)
                    V.tensor_mul(t, scr_b(PR_SLOT[sl], Lc), banks[bA][:, 0:Lc])
                    V.tensor_add(HB[:, dt * TK + c0: dt * TK + c0 + Lc], t,
                                 banks[bB][:, 0:Lc])
                    vm = V.mark(VEC)
                    slot_fence[sl] = (VEC, vm)
                bank_done(bA, VEC, VEC.n)
                bank_done(bB, VEC, VEC.n)
            return VEC.n

        def qkv(th, xn_mark):
            for wdram, outbase in ((qTh_d, QB), (kTh_d, KB)):
                for wc in range(4):
                    wi, ldwn = wload(wdram, wc * 256, 256, rows=D)
                    TN.wait(*ldwn)
                    TN.wait(VEC, xn_mark)
                    for eo in range(2):
                        hl = wc * 2 + eo
                        for tc in range(2):
                            bk = take_bank()
                            for ct in range(DT):
                                TN.matmul(banks[bk][:, 0:NE],
                                          wtile(wi, ct, eo * 128, 128, 256),
                                          HB[:, ct * TK + tc * NE:
                                             ct * TK + (tc + 1) * NE],
                                          start=(ct == 0), stop=(ct == DT - 1))
                            tm = TN.mark(TEN)
                            SC.wait(TEN, tm)
                            SC.emit("activation",
                                    RESb(outbase + hl * T + th * TK + tc * NE,
                                         NE),
                                    banks[bk][:, 0:NE], AF.Copy)
                            bank_done(bk, SCA, SC.mark(SCA))
                    wb_used(wi)
            for wc in range(4):
                wi, ldwn = wload(vTh_d, wc * 256, 256, rows=D)
                TN.wait(*ldwn)
                TN.wait(VEC, xn_mark)
                for tt in range(8):
                    bk = take_bank()
                    for ct in range(DT):
                        TN.matmul(banks[bk][:, 0:256],
                                  HB[:, ct * TK + tt * 128:
                                     ct * TK + (tt + 1) * 128],
                                  WB[wi][:, ct * 256:(ct + 1) * 256],
                                  start=(ct == 0), stop=(ct == DT - 1))
                    tm = TN.mark(TEN)
                    SC.wait(TEN, tm)
                    SC.emit("activation",
                            xs_b((th * 8 + tt) * TK + wc * 256, 256),
                            banks[bk][:, 0:256], AF.Copy)
                    bank_done(bk, SCA, SC.mark(SCA))
                wb_used(wi)

        for th in (0, 1):
            xm = ln3_xn(th)
            qkv(th, xm)
        qkv_done_sc = SCA.n

        TN.wait(SCA, qkv_done_sc)
        TN.wait(*ldw_pre)
        for hl in range(8):
            for qc in range(4):
                bk = take_bank()
                TN.matmul(banks[bk][:, 0:NE], gT_s[:],
                          RESb(QB + hl * T + qc * NE, NE), start=True,
                          stop=True)
                tm = TN.mark(TEN)
                SC.wait(TEN, tm)
                SC.emit("activation", HB[:, hl * T + qc * NE:
                                         hl * T + (qc + 1) * NE],
                        banks[bk][:, 0:NE], AF.Sigmoid, bias=gb_s[:, 0:1])
                bank_done(bk, SCA, SC.mark(SCA))
        gates_sc = SCA.n

        psf = {i: (TEN, 0) for i in range(4)}
        pslot_rr = [0]

        def head_bound(hl):
            for src_base, hc in ((QB, 8), (KB, 12)):
                for qc in range(4):
                    sl = pslot_rr[0] % 4
                    pslot_rr[0] += 1
                    sem, cnt = psf[sl]
                    SC.wait(sem, cnt)
                    SC.emit("activation", scr_b(PR_SLOT[sl], NE),
                            RESb(src_base + hl * T + qc * NE, NE), AF.Square)
                    ms = SC.mark(SCA)
                    bkr = take_bank()
                    TN.wait(SCA, ms)
                    TN.matmul(banks[bkr][0:1, 0:NE], onesb_s[:],
                              scr_b(PR_SLOT[sl], NE), start=True, stop=True)
                    tm = TN.mark(TEN)
                    psf[sl] = (TEN, tm)
                    V.wait(TEN, tm)
                    V.reduce_max(hs_s[0:1, hc + qc:hc + qc + 1],
                                 banks[bkr][0:1, 0:NE], axis=AX.X)
                    vm = V.mark(VEC)
                    V.wait(VEC, vm)
                    bank_done(bkr, VEC, vm)
            V.reduce_max(hs_s[0:1, 0:1], hs_s[0:1, 8:12], axis=AX.X)
            V.mark(VEC)
            V.wait(VEC, VEC.n)
            V.reduce_max(hs_s[0:1, 1:2], hs_s[0:1, 12:16], axis=AX.X)
            V.mark(VEC)
            V.wait(VEC, VEC.n)
            SC.wait(VEC, VEC.n)
            SC.emit("activation", hs_s[0:1, 2:4], hs_s[0:1, 0:2], AF.Ln)
            SC.mark(SCA)
            SC.wait(SCA, SCA.n)
            V.wait(SCA, SCA.n)
            V.tensor_add(hs_s[0:1, 4:5], hs_s[0:1, 2:3], hs_s[0:1, 3:4])
            V.mark(VEC)
            V.wait(VEC, VEC.n)
            SC.wait(VEC, VEC.n)
            SC.emit("activation", hs_s[0:1, 5:6], hs_s[0:1, 4:5], AF.Exp,
                    scale=0.5)
            SC.mark(SCA)
            SC.wait(SCA, SCA.n)
            V.wait(SCA, SCA.n)
            V.tensor_scalar(hs_s[0:1, 6:7], hs_s[0:1, 5:6], SCALE, 1.0,
                            ALU.mult, ALU.add)
            V.mark(VEC)
            V.wait(VEC, VEC.n)
            V.tensor_scalar_mul(hs_s[0:1, 6:7], hs_s[0:1, 6:7], -1.0)
            mb = V.mark(VEC)
            V.wait(VEC, mb)
            bk = take_bank()
            TN.wait(VEC, mb)
            TN.matmul(banks[bk][:, 0:1], ones1f_s[:], hs_s[0:1, 6:7],
                      start=True, stop=True)
            tm = TN.mark(TEN)
            V.wait(TEN, tm)
            V.tensor_copy(mh_s[:, hl:hl + 1], banks[bk][:, 0:1])
            vm = V.mark(VEC)
            V.wait(VEC, vm)
            bank_done(bk, VEC, vm)
            return vm

        rrow_free = {0: (TEN, 0), 1: (TEN, 0)}
        pending = [None]

        def jphase(hl, qc, idx):
            av_bk = take_bank()
            bank_hold.add(av_bk)
            rs_bk = take_bank()
            bank_hold.add(rs_bk)
            jmax = 4 * (qc + 1)
            pend = []

            def emit_scores(j):
                rel = max(0, 128 * j - NE * qc)
                L = NE - rel
                sbk = take_bank()
                TN.matmul(banks[sbk][:, 0:L],
                          RESb(KB + hl * T + j * 128, 128),
                          RESb(QB + hl * T + qc * NE + rel, L),
                          start=True, stop=True)
                tms = TN.mark(TEN)
                sl = pslot_rr[0] % 4
                pslot_rr[0] += 1
                sem, cnt = psf[sl]
                SC.wait(sem, cnt)
                SC.wait(TEN, tms)
                SC.wait(VEC, mh_ready[0])
                SC.emit("activation", scr_b(PR_SLOT[sl], L),
                        banks[sbk][:, 0:L], AF.Exp,
                        bias=mh_s[:, hl:hl + 1], scale=SCALE)
                msp = SC.mark(SCA)
                bank_done(sbk, SCA, msp)
                fence = (SCA, msp)
                if j >= 4 * qc:
                    V.wait(SCA, msp)
                    dcol = 128 * j - NE * qc - rel
                    V.tensor_mul(scr_b(PR_SLOT[sl] + dcol, 128),
                                 scr_b(PR_SLOT[sl] + dcol, 128), tri_s[:])
                    fence = (VEC, V.mark(VEC))
                return (j, rel, L, sl, fence)

            def emit_av(st2):
                j, rel, L, sl, fence = st2
                TN.wait(*fence)
                TN.matmul(banks[av_bk][:, rel:rel + L],
                          xs_b(j * TK + hl * 128, 128),
                          scr_b(PR_SLOT[sl], L),
                          start=(j == 0), stop=(j == jmax - 1))
                TN.matmul(banks[rs_bk][0:1, rel:rel + L], onesb_s[:],
                          scr_b(PR_SLOT[sl], L),
                          start=(j == 0), stop=(j == jmax - 1))
                tm = TN.mark(TEN)
                psf[sl] = (TEN, tm)

            for j in range(jmax):
                pend.append(emit_scores(j))
                if len(pend) >= 2:
                    emit_av(pend.pop(0))
            while pend:
                emit_av(pend.pop(0))
            return {"hl": hl, "qc": qc, "av": av_bk, "rs": rs_bk,
                    "av_tm": TEN.n, "ri": idx % 2}

        def normA(st):
            rrow = rowA if st["ri"] == 0 else rowC
            V.wait(TEN, st["av_tm"])
            V.wait(*rrow_free[st["ri"]])
            V.tensor_copy(rrow[0:1, 0:NE], banks[st["rs"]][0:1, 0:NE])
            vm = V.mark(VEC)
            V.wait(VEC, vm)
            bank_hold.discard(st["rs"])
            bank_done(st["rs"], VEC, vm)
            SC.wait(VEC, vm)
            SC.emit("activation", rowB[0:1, 0:NE], rrow[0:1, 0:NE], AF.Ln)
            SC.mark(SCA)
            SC.wait(SCA, SCA.n)
            SC.emit("activation", rrow[0:1, 0:NE], rowB[0:1, 0:NE], AF.Exp,
                    scale=-1.0)
            st["msr"] = SC.mark(SCA)

        def normB(st):
            hl, qc = st["hl"], st["qc"]
            rrow = rowA if st["ri"] == 0 else rowC
            rb_bk = take_bank()
            TN.wait(SCA, st["msr"])
            TN.matmul(banks[rb_bk][:, 0:NE], ones1f_s[:], rrow[0:1, 0:NE],
                      start=True, stop=True)
            tmr = TN.mark(TEN)
            rrow_free[st["ri"]] = (TEN, tmr)
            V.wait(TEN, tmr)
            V.tensor_copy(rb_s[:, 0:NE], banks[rb_bk][:, 0:NE])
            vm = V.mark(VEC)
            V.wait(VEC, vm)
            bank_done(rb_bk, VEC, vm)
            avc = AVB + hl * T + qc * NE
            V.tensor_mul(xs_b(avc, NE), banks[st["av"]][:, 0:NE],
                         rb_s[:, 0:NE])
            bank_hold.discard(st["av"])
            bank_done(st["av"], VEC, V.mark(VEC))
            V.wait(SCA, gates_sc)
            V.tensor_mul(xs_b(avc, NE), xs_b(avc, NE),
                         HB[:, hl * T + qc * NE: hl * T + (qc + 1) * NE])
            V.mark(VEC)

        idx = 0
        mh_ready = [0]
        for hl in range(8):
            mh_ready[0] = head_bound(hl)
            for qc in range(4):
                st = jphase(hl, qc, idx)
                normA(st)
                if pending[0] is not None:
                    normB(pending[0])
                pending[0] = st
                idx += 1
        normB(pending[0])
        att_done_v = VEC.n

        G.wait(VEC, att_done_v)
        G.wait(SCA, SCA.n)
        for th in (0, 1):
            G.reg_alu(rg["roff"], rg["rmyhalf"], 8 * 128 * TK, ALU.mult)
            G.reg_alu(rg["roff"], rg["roff"], th * 2 * 8 * 128 * TK, ALU.add)
            G.dma_start(out=bass.AP(avsh_d, rg["roff"],
                                    [[TK, 128], [128 * TK, 8], [1, TK]]),
                        in_=XS[:, AVB:AVB + 8 * T]
                        .rearrange("p (hl t) -> p hl t", t=T)
                        [:, :, th * TK:(th + 1) * TK])
            G.mark(GPD, 16)
        pair_signal(GPD.n)
        pair_poll()
        G.wait(TEN, TEN.n)
        G.reg_alu(rg["roff"], rg["rmyhalf"], 2 * 8 * 128 * TK, ALU.mult)
        G.dma_start(out=XS[:, 0:DT * TK]
                    .rearrange("p (ct t) -> p ct t", t=TK),
                    in_=bass.AP(avsh_d, rg["roff"],
                                [[TK, 128], [128 * TK, DT], [1, TK]]))
        avall_ready = G.mark(GPD, 16)
        G.wait(GPD, avall_ready)

        for tc in range(2):
            G.wait(GPD, GPD.n)
            if tc == 1:
                G.wait(VEC, VEC.n)
            G.reg_alu(rg["roff"], rg["rmyhalf"], D * TK, ALU.mult)
            G.reg_alu(rg["roff2"], rg["roff"], tc * NE, ALU.add)
            G.dma_start(out=XS[:].bitcast(F32)
                        [:, DT * NE:DT * NE + DT * NE]
                        .rearrange("p (dt l) -> p dt l", dt=DT),
                        in_=bass.AP(x2sh_d, rg["roff2"],
                                    [[TK, 128], [128 * TK, DT], [1, NE]]))
            x2m = G.mark(GPD, 16)
            for wc in range(8):
                wi, ldwn = wload(wT_d["oT"], wc * 256, 256, rows=D)
                TN.wait(*ldwn)
                TN.wait(GPD, avall_ready)
                for eo in range(2):
                    et = wc * 2 + eo
                    bk = take_bank()
                    for ct in range(DT):
                        TN.matmul(banks[bk][:, 0:NE],
                                  wtile(wi, ct, eo * 128, 128, 256),
                                  xs_b(ct * TK + tc * NE, NE),
                                  start=(ct == 0), stop=(ct == DT - 1))
                    tm = TN.mark(TEN)
                    V.wait(TEN, tm)
                    V.wait(GPD, x2m)
                    V.tensor_add(RES[:, et * TK + tc * NE:
                                     et * TK + (tc + 1) * NE],
                                 banks[bk][:, 0:NE],
                                 XS[:].bitcast(F32)
                                 [:, DT * NE + et * NE:
                                  DT * NE + (et + 1) * NE])
                    bank_done(bk, VEC, V.mark(VEC))
                wb_used(wi)

    V.wait(LDA, ld_pre)
    if upto == "pre":
        emit_output(1, ST1); return _finalize(nc, ctx, (S, G, TN, V, SC))

    def vsync_small(Lc):
        if Lc < 64:
            V.mark(VEC)
            V.wait(VEC, VEC.n)

    def ln0_src(dt):
        return xs_f(dt * TK, TK)

    def ln0_apply(dt, c0, Lc, bkA, bkB):
        V.tensor_mul(res_t(dt, 1 + c0, Lc), ln0_src(dt)[:, c0:c0 + Lc],
                     bkA[:, 0:Lc])
        vsync_small(Lc)
        V.tensor_add(res_t(dt, 1 + c0, Lc), res_t(dt, 1 + c0, Lc),
                     bkB[:, 0:Lc])
        vsync_small(Lc)

    emit_ln(ln0_src, TK, lambda dt: hb_t(dt, 1, TK), ln0_apply)
    if upto == "ln0":
        emit_output(1, ST1); return _finalize(nc, ctx, (S, G, TN, V, SC))

    def ln1_apply(dt, c0, Lc, bkA, bkB):
        t = xs_f(TS32, Lc)
        V.tensor_mul(t, res_t(dt, 1 + c0, Lc), bkA[:, 0:Lc])
        vsync_small(Lc)
        V.tensor_add(t, t, bkB[:, 0:Lc])
        vsync_small(Lc)
        V.tensor_scalar(hb_t(dt, 1 + c0, Lc), t, vec_s["ln1g"][:, dt:dt + 1],
                        vec_s["ln1b"][:, dt:dt + 1], ALU.mult, ALU.add)
        vsync_small(Lc)

    emit_ln(lambda dt: res_t(dt, 1, TK), TK, lambda dt: hb_t(dt, 1, TK),
            ln1_apply)
    if upto == "ln1":
        emit_output(1, ST1); return _finalize(nc, ctx, (S, G, TN, V, SC))

    def col_exchange(sb_col, dram_h, into_view, mask):
        G.wait(VEC, VEC.n)
        G.reg_alu(rg["roff"], rg["rmyhalf"], 128 * DT, ALU.mult)
        G.dma_start(out=bass.AP(dram_h, rg["roff"], [[DT, 128], [1, DT]]),
                    in_=sb_col[:])
        G.mark(GPD, 16)
        pair_signal(GPD.n)
        pair_poll()
        G.reg_alu(rg["roff"], rg["rsibhalf"], 128 * DT, ALU.mult)
        G.dma_start(out=sb_col[:],
                    in_=bass.AP(dram_h, rg["roff"], [[DT, 128], [1, DT]]))
        G.mark(GPD, 16)
        V.wait(GPD, GPD.n)
        if mask:
            V.tensor_scalar_mul(sb_col[:], sb_col[:], maskh_s[:, 0:1])
            V.mark(VEC)
            V.wait(VEC, VEC.n)
        V.tensor_copy(into_view, sb_col[:].rearrange("p (dt o) -> p dt o", o=1))
        m = V.mark(VEC)
        V.wait(VEC, m)
        return m

    pre_k = {0: wload(wT_d["wkT"], 0, WCH), WCH: wload(wT_d["wkT"], WCH, WCH)}
    V.tensor_copy(colbuf_s[:].rearrange("p (dt o) -> p dt o", o=1),
                  HB3()[:, :, TK:TK + 1])
    V.mark(VEC)
    hcol_done = col_exchange(colbuf_s, colsh_d, HB3()[:, :, 0:1], True)
    if upto == "e0a":
        emit_output(1, ST1); return _finalize(nc, ctx, (S, G, TN, V, SC))

    XA, XB = 0, DT * TK
    TCH = ((0, NE), (NE, NE))

    def lerp(mu_name, dst0, wait_ten=None, hbase=0):
        if wait_ten is not None:
            V.wait(TEN, wait_ten)
        V.wait(SCA, SCA.n)
        marks = {}
        for (c0, L) in TCH:
            for dt in range(DT):
                V.tensor_scalar_mul(xs_b(dst0 + dt * TK + c0, L),
                                    hb_t(dt, hbase + 1 + c0, L),
                                    vec_s[mu_name][:, dt:dt + 1])
                V.scalar_tensor_tensor(xs_b(dst0 + dt * TK + c0, L),
                                       hb_t(dt, hbase + c0, L),
                                       omu_s[mu_name][:, dt:dt + 1],
                                       xs_b(dst0 + dt * TK + c0, L),
                                       ALU.mult, ALU.add)
            marks[c0] = V.mark(VEC)
        return marks

    def matpass(wdram, rhs_fn, evac, rt=DT, e_total=DT, rhs_ready=None,
                pre=None):
        for ecol0 in range(0, e_total * 128, WCH):
            if pre and ecol0 in pre:
                wi, ldwn = pre[ecol0]
            else:
                wi, ldwn = wload(wdram, ecol0, WCH, rows=rt * 128)
            TN.wait(*ldwn)
            for eo in range(WCH // 128):
                et = ecol0 // 128 + eo
                for (c0, L) in TCH:
                    if rhs_ready is not None:
                        TN.wait(VEC, rhs_ready[c0] if isinstance(rhs_ready, dict)
                                else rhs_ready[1])
                    bk = take_bank()
                    for ct in range(rt):
                        TN.matmul(banks[bk][:, 0:L],
                                  wtile(wi, ct, eo * 128, 128),
                                  rhs_fn(ct, c0, L),
                                  start=(ct == 0), stop=(ct == rt - 1))
                    tm = TN.mark(TEN)
                    evac(bk, et, c0, L, tm)
            wb_used(wi)

    def xa_rhs(ct, c0, L):
        return xs_b(XA + ct * TK + c0, L)

    def xb_rhs(ct, c0, L):
        return xs_b(XB + ct * TK + c0, L)

    mlp = lerp("muk", XA, wait_ten=None)

    def evac_k(bk, et, c0, L, tm):
        SC.wait(TEN, tm)
        SC.emit("activation", xs_b(XB + et * TK + c0, L), banks[bk][:, 0:L],
                AF.Sigmoid, bias=ubias2_s[:, et:et + 1])
        bank_done(bk, SCA, SC.mark(SCA))

    matpass(wT_d["wkT"], xa_rhs, evac_k, rhs_ready=mlp, pre=pre_k)
    kpass_ten = TEN.n

    mlp = lerp("mur", XA, wait_ten=kpass_ten)
    slot_state = {0: 0, 1: 0, 2: 0, 3: 0}
    slot_rr = [0]

    def evac_r(bk, et, c0, L, tm):
        sl = slot_rr[0] % 4
        slot_rr[0] += 1
        SC.wait(TEN, tm)
        SC.wait(VEC, slot_state[sl])
        SC.emit("activation", scr_b(PR_SLOT[sl], L), banks[bk][:, 0:L],
                AF.Sigmoid)
        ms = SC.mark(SCA)
        bank_done(bk, SCA, ms)
        V.wait(SCA, ms)
        V.tensor_mul(xs_b(XB + et * TK + c0, L), scr_b(PR_SLOT[sl], L),
                     xs_b(XB + et * TK + c0, L))
        slot_state[sl] = V.mark(VEC)

    matpass(wT_d["wrT"], xa_rhs, evac_r, rhs_ready=mlp)
    rpass_ten = TEN.n

    mlp = lerp("muv", XA, wait_ten=rpass_ten)

    def evac_v(bk, et, c0, L, tm):
        V.wait(TEN, tm)
        V.tensor_mul(xs_b(XB + et * TK + c0, L), banks[bk][:, 0:L],
                     xs_b(XB + et * TK + c0, L))
        bank_done(bk, VEC, V.mark(VEC))

    matpass(wT_d["wvT"], xa_rhs, evac_v, rhs_ready=mlp)
    vpass_vec = VEC.n

    def evac_o(bk, et, c0, L, tm):
        V.wait(TEN, tm)
        V.tensor_add(res_t(et, 1 + c0, L), res_t(et, 1 + c0, L),
                     banks[bk][:, 0:L])
        bank_done(bk, VEC, V.mark(VEC))

    matpass(wT_d["woT"], xb_rhs, evac_o, rhs_ready={0: vpass_vec, NE: vpass_vec})
    if upto == "tm":
        emit_output(1, ST1); return _finalize(nc, ctx, (S, G, TN, V, SC))

    V.wait(VEC, VEC.n)
    V.tensor_copy(colbuf2_s[:].rearrange("p (dt o) -> p dt o", o=1),
                  RES3()[:, :, TK:TK + 1])
    V.mark(VEC)
    col_exchange2_done = col_exchange(colbuf2_s, colxm_d, RES3()[:, :, 0:1],
                                      False)

    def ln2_apply(dt, c0, Lc, bkA, bkB):
        t = xs_f(TS32, Lc)
        V.tensor_mul(t, res_t(dt, c0, Lc), bkA[:, 0:Lc])
        vsync_small(Lc)
        V.tensor_add(t, t, bkB[:, 0:Lc])
        vsync_small(Lc)
        V.tensor_scalar(hb_t(dt, c0, Lc), t, vec_s["ln2g"][:, dt:dt + 1],
                        vec_s["ln2b"][:, dt:dt + 1], ALU.mult, ALU.add)
        vsync_small(Lc)

    V.wait(VEC, col_exchange2_done)
    emit_ln(lambda dt: res_t(dt, 0, ST1), ST1, lambda dt: hb_t(dt, 0, ST1),
            ln2_apply, debug_tap=2)
    V.tensor_scalar_mul(HB3()[:, :, 0:1], HB3()[:, :, 0:1], maskh_s[:, 0:1])
    h2_done = V.mark(VEC)
    if upto == "ln2rows":
        G.wait(VEC, VEC.n)
        G.wait(SCA, SCA.n)
        G.dma_start(out=yout_d.ap()[0:1, 0:TK], in_=rowA[0:1, 0:TK])
        G.mark(STO, 16)
        G.dma_start(out=yout_d.ap()[1:2, 0:TK], in_=rowB[0:1, 0:TK])
        G.mark(STO, 16)
        G.dma_start(out=yout_d.ap()[2:3, 0:2], in_=rowA[0:1, 1023:1025])
        G.mark(STO, 16)
        G.dma_start(out=yout_d.ap()[3:4, 0:2], in_=rowB[0:1, 1023:1025])
        G.mark(STO, 16)
        G.dma_start(out=yout_d.ap()[4:5, 0:4], in_=hs_s[0:1, 0:4])
        G.mark(STO, 16)
        G.wait(STO, STO.n)
        S.wait(STO, STO.n)
        return _finalize(nc, ctx, (S, G, TN, V, SC))
    if upto == "h2":
        G.wait(VEC, VEC.n)
        G.wait(SCA, SCA.n)
        G.dma_start(out=yout_d.ap().rearrange("(dt p) l -> p dt l", p=128),
                    in_=HB3()[:, :, 1:1 + TK])
        G.mark(STO, 16)
        G.wait(STO, STO.n)
        S.wait(STO, STO.n)
        return _finalize(nc, ctx, (S, G, TN, V, SC))

    V.wait(VEC, h2_done)
    lerp("cmuk", XA)
    m_lerp2 = lerp("cmur", XB)

    def evac_k2(bk, et, c0, L, tm):
        sl = slot_rr[0] % 4
        slot_rr[0] += 1
        SC.wait(TEN, tm)
        SC.wait(VEC, slot_state[sl])
        SC.emit("activation", scr_b(PR_SLOT[sl], L), banks[bk][:, 0:L],
                AF.Relu)
        mr = SC.mark(SCA)
        SC.wait(SCA, mr)
        SC.emit("activation", hb_t(et, 1 + c0, L), scr_b(PR_SLOT[sl], L),
                AF.Square)
        ms = SC.mark(SCA)
        slot_state[sl] = ms
        bank_done(bk, SCA, ms)

    matpass(wT_d["cwkT"], xa_rhs, evac_k2, rhs_ready=m_lerp2)

    def evac_r2(bk, et, c0, L, tm):
        SC.wait(TEN, tm)
        SC.emit("activation", xs_b(XA + et * TK + c0, L), banks[bk][:, 0:L],
                AF.Sigmoid)
        bank_done(bk, SCA, SC.mark(SCA))

    matpass(wT_d["cwrT"], xb_rhs, evac_r2)

    def k2_rhs(ct, c0, L):
        return hb_t(ct, 1 + c0, L)

    def evac_cm(bk, et, c0, L, tm):
        t = xs_f(TS32, L)
        V.wait(TEN, tm)
        V.tensor_mul(t, banks[bk][:, 0:L], xs_b(XA + et * TK + c0, L))
        bank_done(bk, VEC, V.mark(VEC))
        V.tensor_add(res_t(et, 1 + c0, L), res_t(et, 1 + c0, L), t)
        V.mark(VEC)

    matpass(wT_d["cwvT"], k2_rhs, evac_cm)
    cm_ten = TEN.n

    for (c0, L) in TCH:
        S.wait(TEN, cm_ten)
        S.wait(VEC, VEC.n)
        dst = XS[:].bitcast(F32)[:, DT * NE: DT * NE + DT * L] \
            .rearrange("p (dt l) -> p dt l", dt=DT)
        S.dma_start(out=dst,
                    in_=x0T_d.ap()[:, c0:c0 + L]
                    .rearrange("(dt p) l -> p dt l", p=128))
        mx = S.mark(LDA, 16)
        V.wait(LDA, mx)
        for dt in range(DT):
            V.tensor_add(res_t(dt, 1 + c0, L), res_t(dt, 1 + c0, L),
                         XS[:].bitcast(F32)[:, DT * NE + dt * L:
                                            DT * NE + dt * L + L])
        V.mark(VEC)
    s1_done = V.mark(VEC)

    if upto == "s1":
        emit_output(1, ST1)
        return _finalize(nc, ctx, (S, G, TN, V, SC))

    r = emit_moe(lambda dt: res_t(dt, 1, TK))
    if r == "STOP":
        G.wait(STO, STO.n)
        S.wait(STO, STO.n)
        return _finalize(nc, ctx, (S, G, TN, V, SC))
    if upto == "s2":
        emit_output(1, ST1)
        return _finalize(nc, ctx, (S, G, TN, V, SC))

    emit_attention()
    if upto == "s3":
        emit_output(0, TK)
        return _finalize(nc, ctx, (S, G, TN, V, SC))

    emit_moe(lambda dt: RES[:, dt * TK: (dt + 1) * TK])
    emit_output(0, TK)
    return _finalize(nc, ctx, (S, G, TN, V, SC))


def _finalize(nc, ctx, streams):
    S, G, TN, V, SC = streams
    with nc.Block() as block:
        def runner(st):
            def run(eng):
                guards = []
                for r in st.q:
                    if isinstance(r, tuple):
                        if r[0] == "WHILE_BEGIN":
                            g = eng.While(r[1])
                            g.__enter__()
                            guards.append(g)
                        else:
                            guards.pop().__exit__(None, None, None)
                        continue
                    inst = getattr(eng, r.op)(*r.args, **r.kwargs)
                    for sem, k in r.incs:
                        inst.then_inc(sem.h if isinstance(sem, Sem) else sem, k)
            return run
        block.sync(runner(S))
        block.gpsimd(runner(G))
        block.tensor(runner(TN))
        block.vector(runner(V))
        block.scalar(runner(SC))
    ctx.close()
    return nc


def stage_inputs(inputs, upto="full"):
    f = np.float32
    x = np.asarray(inputs["L0_attn_a_x"], dtype=f)

    def t2(a):
        return np.ascontiguousarray(np.asarray(a, dtype=f).T)

    def vec128(a):
        return np.ascontiguousarray(np.asarray(a, dtype=f).reshape(DT, 128).T)

    wTs = {
        "wrT": t2(inputs["L0_attn_a_tm_Wr"]),
        "wkT": t2(inputs["L0_attn_a_tm_Wk"]),
        "wvT": t2(inputs["L0_attn_a_tm_Wv"]),
        "woT": t2(inputs["L0_attn_a_tm_Wout"]),
        "cwrT": t2(inputs["L0_attn_a_cm_Wr"]),
        "cwkT": t2(inputs["L0_attn_a_cm_Wk"]),
        "cwvT": t2(inputs["L0_attn_a_cm_Wv"]),
        "oT": t2(inputs["L0_attn_b_o_w"]),
    }
    gT = t2(inputs["L0_attn_b_gate_w"])
    gb = np.ascontiguousarray(
        np.asarray(inputs["L0_attn_b_gate_b"], dtype=f).reshape(HD, 1))
    rT = np.ascontiguousarray(
        np.asarray(inputs["L0_ffn_router_w"], dtype=f).T
        .reshape(DT, 128, E).transpose(1, 0, 2).reshape(128, DT * E))
    w1 = np.ascontiguousarray(np.asarray(inputs["L0_ffn_w1"], dtype=f))
    w3 = np.ascontiguousarray(np.asarray(inputs["L0_ffn_w3"], dtype=f))
    w2 = np.ascontiguousarray(np.asarray(inputs["L0_ffn_w2"], dtype=f))
    vecs = {
        "ln1g": vec128(inputs["L0_attn_a_ln1_g"]),
        "ln1b": vec128(inputs["L0_attn_a_ln1_b"]),
        "ln2g": vec128(inputs["L0_attn_a_ln2_g"]),
        "ln2b": vec128(inputs["L0_attn_a_ln2_b"]),
        "mur": vec128(inputs["L0_attn_a_tm_mu_r"]),
        "muk": vec128(inputs["L0_attn_a_tm_mu_k"]),
        "muv": vec128(inputs["L0_attn_a_tm_mu_v"]),
        "cmur": vec128(inputs["L0_attn_a_cm_mu_r"]),
        "cmuk": vec128(inputs["L0_attn_a_cm_mu_k"]),
        "ubias": vec128(inputs["L0_attn_a_tm_u"]),
    }
    idf = np.eye(128, dtype=ml_dtypes.bfloat16)
    idf32 = np.eye(128, dtype=f)
    sel8 = np.zeros((E, E * 128), dtype=ml_dtypes.bfloat16)
    for e in range(E):
        sel8[e, e * 128:(e + 1) * 128] = 1.0
    tri = np.tril(np.ones((128, 128), dtype=f)).T
    tri = tri.astype(ml_dtypes.bfloat16)
    rng = np.random.default_rng()
    nonce = int(rng.integers(1, 2**31 - 16))
    flagv = np.arange(1, 9, dtype=np.uint32) + np.uint32(nonce)
    flagv = flagv.reshape(1, 8)

    in_maps = []
    for c in range(8):
        bb, half = c // 2, c % 2
        xt = np.ascontiguousarray(x[bb, half * TK:(half + 1) * TK, :].T)
        qfull = np.asarray(inputs["L0_attn_b_q_w"], dtype=f).T
        kfull = np.asarray(inputs["L0_attn_b_k_w"], dtype=f).T
        vfull = np.asarray(inputs["L0_attn_b_v_w"], dtype=f).T
        sl = slice(half * 1024, (half + 1) * 1024)
        qTh = np.ascontiguousarray(qfull[:, sl])
        kTh = np.ascontiguousarray(kfull[:, sl])
        vTh = np.ascontiguousarray(vfull[:, sl])
        m = {
            "x0T": xt, "qTh": qTh, "kTh": kTh, "vTh": vTh,
            "gT": gT, "gb": gb, "rT": rT,
            "w1": w1, "w3": w3, "w2": w2,
            "maskh": np.full((128, 1), float(half), dtype=f),
            "idf": idf, "idf32": idf32, "sel8": sel8, "tri": tri,
            "flagv": flagv,
            "hinfo": np.array([[half, 1 - half, 0, 0]], dtype=np.uint32),
        }
        m.update(wTs)
        m.update(vecs)
        in_maps.append(m)
    return in_maps


_NC_CACHE = {}


def kernel(_upto=None, **inputs):
    if _upto is None:
        _upto = os.environ.get("K_UPTO", "full")
    upto = _upto
    in_maps = stage_inputs(inputs, upto)
    if upto not in _NC_CACHE:
        _NC_CACHE[upto] = build_program(upto)
    nc = _NC_CACHE[upto]
    res = run_bass_kernel_spmd(nc, in_maps, core_ids=list(range(8)))
    out = np.zeros((B, T, D), dtype=np.float32)
    for c in range(8):
        bb, half = c // 2, c % 2
        out[bb, half * TK:(half + 1) * TK, :] = res.results[c]["yout"].T
    return out


if __name__ == "__main__":
    pass


# version 07
# speedup vs baseline: 1.1090x; 1.1090x over previous
import math
import os
from contextlib import ExitStack

import numpy as np
import ml_dtypes
import concourse.bass as bass
import concourse.mybir as mybir
from concourse.bass_utils import run_bass_kernel_spmd

F32 = mybir.dt.float32
BF16 = mybir.dt.bfloat16
U32 = mybir.dt.uint32
AF = mybir.ActivationFunctionType
ALU = mybir.AluOpType
AX = mybir.AxisListType

B, T, D = 4, 2048, 2048
DT = D // 128
TK = 1024
ST1 = TK + 1
H, HD = 16, 128
E, FF = 8, 512
EPS = 1e-5
LN1E8 = 18.420680743952367
SCALE = 1.0 / math.sqrt(HD)
NE = 512
WCH = 256

ENGS = ("sync", "gpsimd", "tensor", "vector", "scalar")


class Rec:
    __slots__ = ("op", "args", "kwargs", "incs")

    def __init__(self, op, args, kwargs):
        self.op, self.args, self.kwargs = op, args, kwargs
        self.incs = []

    def then_inc(self, sem, n):
        for i, (s0, n0) in enumerate(self.incs):
            if s0 is sem:
                self.incs[i] = (s0, n0 + n)
                return self
        self.incs.append((sem, n))
        return self


class Sem:
    def __init__(self, handle, name):
        self.h = handle
        self.name = name
        self.n = 0


AUTO_OPS = {
    "tensor_copy", "tensor_tensor", "tensor_add", "tensor_sub", "tensor_mul",
    "tensor_max", "tensor_scalar", "tensor_single_scalar", "tensor_scalar_add",
    "tensor_scalar_sub", "tensor_scalar_mul", "tensor_scalar_max",
    "tensor_scalar_min", "tensor_relu", "tensor_reduce", "reduce_max",
    "reduce_sum", "scalar_tensor_tensor", "reciprocal", "memset", "select",
    "activation", "copy", "add", "mul",
}


class Stream:
    def __init__(self, name):
        self.name = name
        self.q = []
        self.wait_cache = {}
        self.auto_sem = None
        self.auto_n = 0

    def emit(self, op, *args, **kwargs):
        if self.auto_sem is not None and op in AUTO_OPS:
            if self.auto_sem.n > 0:
                self.q.append(Rec("wait_ge", (self.auto_sem.h, self.auto_sem.n),
                                  {}))
            r = Rec(op, args, kwargs)
            self.q.append(r)
            r.then_inc(self.auto_sem, 1)
            self.auto_sem.n += 1
            return r
        r = Rec(op, args, kwargs)
        self.q.append(r)
        return r

    def __getattr__(self, op):
        if op.startswith("_"):
            raise AttributeError(op)
        return lambda *a, **k: self.emit(op, *a, **k)

    def wait(self, sem, n=None):
        if n is None:
            n = sem.n
        if n <= 0:
            return
        if self.wait_cache.get(id(sem), -1) >= n:
            return
        self.wait_cache[id(sem)] = n
        self.emit("wait_ge", sem.h, n)

    def mark(self, sem, k=1):
        self.q[-1].then_inc(sem, k)
        sem.n += k
        return sem.n

    def while_begin(self, reg):
        self.q.append(("WHILE_BEGIN", reg))

    def while_end(self):
        self.q.append(("WHILE_END",))


def chunks_of(total, size=NE):
    out, p = [], 0
    while p < total:
        out.append((p, min(size, total - p)))
        p += size
    return out


def build_program(upto="full"):
    nc = bass.Bass("TRN2", target_bir_lowering=False)
    ctx = ExitStack()
    S, G, TN, V, SC = (Stream(n) for n in ENGS)

    sems = {}
    for name in ("LDA", "LDW", "TEN", "VEC", "SCA", "STO", "POL", "GPD",
                 "SPD", "LW0", "LW1", "LDS", "XS0", "XS1", "XS2", "XS3"):
        sems[name] = Sem(ctx.enter_context(nc.semaphore(name)), name)
    LDA, LDW, TEN, VEC, SCA, STO, POL, GPD = (sems[k] for k in
        ("LDA", "LDW", "TEN", "VEC", "SCA", "STO", "POL", "GPD"))
    LW = (sems["LW0"], sems["LW1"])
    XSEM = (sems["XS0"], sems["XS1"], sems["XS2"], sems["XS3"])

    V.auto_sem = None
    SC.auto_sem = None

    def din(name, shape, dtype=F32):
        return nc.dram_tensor(name, list(shape), dtype, kind="ExternalInput")

    x0T_d = din("x0T", [D, TK])
    wT_d = {k: din(k, [D, D]) for k in
            ("wrT", "wkT", "wvT", "woT", "cwrT", "cwkT", "cwvT", "oT")}
    qTh_d = din("qTh", [D, 8 * HD])
    kTh_d = din("kTh", [D, 8 * HD])
    vTh_d = din("vTh", [D, 8 * HD])
    gT_d = din("gT", [HD, HD])
    rT_d = din("rT", [128, DT * E])
    w1_d = din("w1", [E, D, FF])
    w3_d = din("w3", [E, D, FF])
    w2_d = din("w2", [E, FF, D])
    vec_names = ("ln1g", "ln1b", "ln2g", "ln2b", "mur", "muk", "muv",
                 "cmur", "cmuk", "ubias")
    vec_d = {k: din(k, [128, DT]) for k in vec_names}
    gb_d = din("gb", [HD, 1])
    maskh_d = din("maskh", [128, 1])
    idf_d = din("idf", [128, 128], BF16)
    idf32_d = din("idf32", [128, 128])
    sel8_d = din("sel8", [E, E * 128], BF16)
    tri_d = din("tri", [128, 128], BF16)
    flagv_d = din("flagv", [1, 8], U32)
    hinfo_d = din("hinfo", [1, 4], U32)
    yout_d = nc.dram_tensor("yout", [D, TK], F32, kind="ExternalOutput")

    colsh_d = nc.dram_tensor("colsh", [2, 128, DT], BF16, kind="Internal",
                             addr_space="Shared")
    colxm_d = nc.dram_tensor("colxm", [2, 128, DT], F32, kind="Internal",
                             addr_space="Shared")
    flsh_d = nc.dram_tensor("flsh", [1, 16], U32, kind="Internal",
                            addr_space="Shared")
    if upto not in ("s1",):
        x2sh_d = nc.dram_tensor("x2sh", [2, D, TK], F32, kind="Internal",
                                addr_space="Shared")
        avsh_d = nc.dram_tensor("avsh", [2, 2 * 8 * 128, TK], BF16,
                                kind="Internal", addr_space="Shared")

    def sb(name, shape, dtype):
        return ctx.enter_context(nc.sbuf_tensor(name, list(shape), dtype))

    vec_s = {k: sb("s_" + k, [128, DT], F32) for k in vec_names}
    gb_s = sb("s_gb", [HD, 1], F32)
    maskh_s = sb("s_maskh", [128, 1], F32)
    idf_s = sb("s_idf", [128, 128], BF16)
    idf32_s = sb("s_idf32", [128, 128], F32)
    rTf_s = sb("s_rTf", [128, DT * E], F32)
    onesf_s = sb("s_onesf", [128, 1], F32)
    col_s = sb("s_col", [128, 1], F32)
    sel8_s = sb("s_sel8", [E, E * 128], BF16)
    tri_s = sb("s_tri", [128, 128], BF16)
    gT_s = sb("s_gT", [HD, HD], BF16)
    flagv_s = sb("s_flagv", [1, 8], U32)
    hinfo_s = sb("s_hinfo", [1, 4], U32)
    sflag_s = sb("s_sflag", [1, 1], U32)
    onesb_s = sb("s_onesb", [128, 1], BF16)
    ones1f_s = sb("s_ones1f", [1, 128], F32)
    ubias2_s = sb("s_ubias2", [128, DT], F32)
    omu_s = {k: sb("s_omu_" + k, [128, DT], F32)
             for k in ("mur", "muk", "muv", "cmur", "cmuk")}
    colbuf_s = sb("s_colbuf", [128, DT], BF16)
    colbuf2_s = sb("s_colbuf2", [128, DT], F32)
    rowA = sb("s_rowA", [1, ST1 + 31], F32)
    rowB = sb("s_rowB", [1, ST1 + 31], F32)
    rowC = sb("s_rowC", [1, ST1 + 31], F32)
    hs_s = sb("s_hs", [1, 64], F32)
    mh_s = sb("s_mh", [128, 8], F32)
    sc8_s = sb("s_sc8", [128, 64], F32)
    sc8b_s = sb("s_sc8b", [128, 64], F32)
    wfin_s = sb("s_wfin", [128, 64], BF16)
    top2_s = sb("s_top2", [128, 64], F32)
    top2b_s = sb("s_top2b", [128, 64], F32)
    wTs_s = sb("s_wTs", [8, TK], BF16)
    wb_s = sb("s_wb", [128, TK], BF16)
    rb_s = sb("s_rb", [128, NE], F32)

    RES = sb("RES", [128, DT * ST1], F32)
    HB = sb("HB", [128, DT * ST1], BF16)
    XS = sb("XS", [128, 2 * DT * TK], BF16)
    WB = [sb(f"WB{i}", [128, DT * WCH], BF16) for i in range(2)]
    SCR = sb("SCR", [128, 2112], BF16)

    banks = [ctx.enter_context(nc.psum_tensor(f"pb{i}", [128, NE], F32))
             for i in range(7)]
    pT8 = ctx.enter_context(nc.psum_tensor("pT8", [8, 128], BF16))

    bank_state = [(VEC, 0)] * 7
    bank_rr = [0]

    bank_hold = set()

    def take_bank():
        while True:
            i = bank_rr[0] % 7
            bank_rr[0] += 1
            if i not in bank_hold:
                break
        sem, cnt = bank_state[i]
        TN.wait(sem, cnt)
        return i

    def bank_done(i, sem, cnt=None):
        bank_state[i] = (sem, cnt if cnt is not None else sem.n)

    rg = {k: nc.gpsimd.alloc_register(k) for k in
          ("roff", "roff2", "rexp", "rflag", "rcond", "rtgt", "rmyhalf",
           "rsibhalf")}
    sg = {k: nc.sync.alloc_register("s_" + k) for k in
          ("roff", "rexp", "rflag", "rcond", "rtgt", "rmyhalf", "rsibhalf")}

    def RES3():
        return RES[:].rearrange("p (dt l) -> p dt l", dt=DT)

    def HB3():
        return HB[:].rearrange("p (dt l) -> p dt l", dt=DT)

    def res_t(dt, c0, L):
        return RES[:, dt * ST1 + c0: dt * ST1 + c0 + L]

    def hb_t(dt, c0, L):
        return HB[:, dt * ST1 + c0: dt * ST1 + c0 + L]

    def xs_b(c0, L):
        return XS[:, c0:c0 + L]

    def xs_f(c0f, L):
        return XS[:].bitcast(F32)[:, c0f:c0f + L]

    def scr_b(c0, L):
        return SCR[:, c0:c0 + L]

    def scr_f(c0f, L):
        return SCR[:].bitcast(F32)[:, c0f:c0f + L]

    SQ_SLOT = (0, 1056)
    PR_SLOT = (0, 512, 1024, 1536)
    TS32 = 8192

    def sdma(out, in_):
        S.dma_start(out=out, in_=in_)
        S.mark(LDA, 16)
        return LDA.n

    for k in vec_names:
        sdma(vec_s[k][:], vec_d[k].ap())
    sdma(gb_s[:], gb_d.ap())
    sdma(maskh_s[:], maskh_d.ap())
    sdma(idf_s[:], idf_d.ap())
    sdma(idf32_s[:], idf32_d.ap())
    sdma(rTf_s[:], rT_d.ap())
    sdma(sel8_s[:], sel8_d.ap())
    sdma(flagv_s[:], flagv_d.ap())
    sdma(hinfo_s[:], hinfo_d.ap())
    sdma(XS[:].bitcast(F32).rearrange("p (dt l) -> p dt l", dt=DT),
         x0T_d.ap().rearrange("(dt p) l -> p dt l", p=128))
    ld_pre = LDA.n
    G.dma_start(out=gT_s[:], in_=gT_d.ap())
    G.mark(sems["LDS"], 16)
    G.dma_start(out=tri_s[:], in_=tri_d.ap())
    G.mark(sems["LDS"], 16)
    ldw_pre = (sems["LDS"], sems["LDS"].n)

    V.memset(onesb_s[:], 1.0)
    V.mark(VEC)
    V.memset(ones1f_s[:], 1.0)
    V.mark(VEC)
    V.memset(onesf_s[:], 1.0)
    V.mark(VEC)
    V.wait(LDA, ld_pre)
    V.tensor_scalar_add(ubias2_s[:], vec_s["ubias"][:], LN1E8)
    V.mark(VEC)
    for k in ("mur", "muk", "muv", "cmur", "cmuk"):
        V.tensor_scalar(omu_s[k][:], vec_s[k][:], -1.0, 1.0, ALU.mult, ALU.add)
        V.mark(VEC)
    V.wait(VEC, VEC.n)

    G.wait(LDA, ld_pre)
    G.reg_load(rg["rmyhalf"], hinfo_s[0:1, 0:1])
    G.reg_load(rg["rsibhalf"], hinfo_s[0:1, 1:2])
    G.reg_mov(rg["rtgt"], 0)
    S.wait(LDA, ld_pre)
    S.reg_load(sg["rmyhalf"], hinfo_s[0:1, 0:1])
    S.reg_load(sg["rsibhalf"], hinfo_s[0:1, 1:2])
    S.reg_mov(sg["rtgt"], 0)

    fl_phase = [0]

    def pair_signal(payload_sem_target):
        ph = fl_phase[0]
        S.wait(GPD, payload_sem_target)
        S.reg_alu(sg["roff"], sg["rmyhalf"], ph * 2, ALU.add)
        S.dma_start(out=bass.AP(flsh_d, sg["roff"], [[1, 1], [1, 1]]),
                    in_=flagv_s[0:1, ph:ph + 1])
        S.mark(SPD := sems["SPD"], 16)

    def pair_poll():
        ph = fl_phase[0]
        S.reg_alu(sg["roff"], sg["rsibhalf"], ph * 2, ALU.add)
        S.reg_load(sg["rexp"], flagv_s[0:1, ph:ph + 1])
        S.reg_mov(sg["rcond"], 1)
        S.while_begin(sg["rcond"])
        S.dma_start(out=sflag_s[:],
                    in_=bass.AP(flsh_d, sg["roff"], [[1, 1], [1, 1]]))
        S.q[-1].then_inc(POL, 16)
        S.reg_alu(sg["rtgt"], sg["rtgt"], 16, ALU.add)
        S.emit("wait_ge", POL.h, sg["rtgt"])
        S.reg_load(sg["rflag"], sflag_s[0:1, 0:1])
        S.reg_alu(sg["rcond"], sg["rflag"], sg["rexp"], ALU.not_equal)
        S.while_end()
        S.emit("nop")
        S.mark(sems["SPD"], 1)
        fl_phase[0] += 1
        return sems["SPD"].n

    wb_state = [(TEN, 0), (TEN, 0)]
    wb_rr = [0]

    def wload(dram_h, col0, cw, rows=D, expert=None):
        i = wb_rr[0] % 2
        wb_rr[0] += 1
        sem, cnt = wb_state[i]
        G.wait(sem, cnt)
        rt = rows // 128
        src = dram_h.ap()
        if expert is not None:
            src = src[expert]
        src = src[:, col0:col0 + cw].rearrange("(rt p) e -> p rt e", p=128)
        dst = WB[i][:, 0:rt * cw].rearrange("p (rt e) -> p rt e", rt=rt)
        G.dma_start(out=dst, in_=src)
        G.mark(LW[i], 16)
        return i, (LW[i], LW[i].n)

    def wtile(i, ct, e0, ew, cw=WCH):
        return WB[i][:, ct * cw + e0: ct * cw + e0 + ew]

    def wb_used(i):
        wb_state[i] = (TEN, TEN.n)

    def emit_rows(b1, b2, c0, Lc, stat_done):
        rA, rB, rC = rowA[0:1, c0:c0 + Lc], rowB[0:1, c0:c0 + Lc], \
            rowC[0:1, c0:c0 + Lc]
        V.wait(TEN, stat_done)
        V.tensor_scalar_mul(rC, banks[b1][0:1, 0:Lc], 1.0 / D)
        mC = V.mark(VEC)
        V.tensor_scalar(rA, banks[b2][0:1, 0:Lc], 1.0 / D, EPS,
                        ALU.mult, ALU.add)
        V.mark(VEC)
        V.wait(VEC, VEC.n)
        V.tensor_mul(rB, rC, rC)
        V.mark(VEC)
        V.wait(VEC, VEC.n)
        V.tensor_sub(rA, rA, rB)
        mV = V.mark(VEC)
        bank_done(b1, VEC, mC)
        bank_done(b2, VEC, mV)
        SC.wait(VEC, mV)
        SC.emit("activation", rB, rA, AF.Ln)
        SC.mark(SCA)
        SC.wait(SCA, SCA.n)
        SC.emit("activation", rA, rB, AF.Exp, scale=-0.5)
        mS = SC.mark(SCA)
        V.wait(SCA, mS)
        V.tensor_mul(rB, rC, rA)
        V.mark(VEC)
        V.wait(VEC, VEC.n)
        V.tensor_scalar_mul(rB, rB, -1.0)
        mB = V.mark(VEC)
        V.wait(VEC, mB)
        bA = take_bank()
        TN.wait(SCA, mS)
        TN.matmul(banks[bA][:, 0:Lc], ones1f_s[:], rA, start=True, stop=True)
        bB = take_bank()
        TN.wait(VEC, mB)
        TN.matmul(banks[bB][:, 0:Lc], ones1f_s[:], rB, start=True, stop=True)
        mT = TN.mark(TEN)
        return (c0, Lc, bA, bB, mT)

    def emit_ln(src_fn, L, cast_fn, apply_fn, debug_tap=None):
        ch = chunks_of(L)
        stat_banks = [(take_bank(), take_bank()) for _ in ch]
        cast_marks, sq_marks, mm_marks = [], [], []
        for dt in range(DT):
            V.tensor_copy(cast_fn(dt), src_fn(dt))
            cast_marks.append(V.mark(VEC))
            if dt >= 2:
                SC.wait(TEN, mm_marks[dt - 2])
            SC.wait(VEC, cast_marks[dt])
            SC.emit("activation", scr_b(SQ_SLOT[dt % 2], L), cast_fn(dt),
                    AF.Square)
            sq_marks.append(SC.mark(SCA))
            TN.wait(VEC, cast_marks[dt])
            TN.wait(SCA, sq_marks[dt])
            for ci, (c0, Lc) in enumerate(ch):
                b1, b2 = stat_banks[ci]
                TN.matmul(banks[b1][0:1, 0:Lc], onesb_s[:],
                          cast_fn(dt)[:, c0:c0 + Lc],
                          start=(dt == 0), stop=(dt == DT - 1))
                TN.matmul(banks[b2][0:1, 0:Lc], onesb_s[:],
                          scr_b(SQ_SLOT[dt % 2] + c0, Lc),
                          start=(dt == 0), stop=(dt == DT - 1))
            mm_marks.append(TN.mark(TEN))
        stat_done = TEN.n
        if debug_tap is not None:
            b1d, b2d = stat_banks[debug_tap]
            c0d, Ld = ch[debug_tap]
            V.wait(TEN, stat_done)
            V.tensor_copy(hs_s[0:1, 0:2], banks[b1d][0:1, 0:2])
            V.mark(VEC)
            V.tensor_copy(hs_s[0:1, 2:4], banks[b2d][0:1, 0:2])
            V.mark(VEC)
        out_chunks = []
        for ci, (c0, Lc) in enumerate(ch):
            b1, b2 = stat_banks[ci]
            out_chunks.append(emit_rows(b1, b2, c0, Lc, stat_done))
        for (c0, Lc, bA, bB, mT) in out_chunks:
            V.wait(TEN, mT)
            for dt in range(DT):
                apply_fn(dt, c0, Lc, banks[bA], banks[bB])
            mdone = V.mark(VEC)
            bank_done(bA, VEC, mdone)
            bank_done(bB, VEC, mdone)

    def emit_output(col0, stride):
        S.wait(VEC, VEC.n)
        S.wait(SCA, SCA.n)
        if stride == ST1:
            src = RES3()[:, :, col0:col0 + TK]
        else:
            src = RES[:, 0:DT * TK].rearrange("p (dt l) -> p dt l", dt=DT)
        S.dma_start(out=yout_d.ap().rearrange("(dt p) l -> p dt l", p=128),
                    in_=src)
        S.mark(STO, 16)
        S.wait(STO, STO.n)
        G.wait(STO, STO.n)



    def xn_t(dt):
        return HB[:, dt * TK: dt * TK + TK]

    def emit_moe(res_fn):
        def moe_apply(dt, c0, Lc, bkA, bkB):
            t = xs_f(TS32, Lc)
            V.tensor_mul(t, res_fn(dt)[:, c0:c0 + Lc], bkA[:, 0:Lc])
            vsync_small(Lc)
            V.tensor_add(xn_t(dt)[:, c0:c0 + Lc], t, bkB[:, 0:Lc])
            vsync_small(Lc)

        emit_ln(res_fn, TK, xn_t, moe_apply)
        xn_done = V.mark(VEC)

        TN.wait(LDA, ld_pre)
        bkr = take_bank()
        for ct in range(DT):
            TN.matmul(banks[bkr][0:1, 0:E], onesf_s[:],
                      rTf_s[:, ct * E:(ct + 1) * E],
                      start=(ct == 0), stop=(ct == DT - 1))
        tm = TN.mark(TEN)
        V.wait(TEN, tm)
        V.tensor_copy(hs_s[0:1, 8:16], banks[bkr][0:1, 0:E])
        m_rsum = V.mark(VEC)
        bank_done(bkr, VEC, m_rsum)
        for tc in range(2):
            bkr = take_bank()
            for ct in range(DT):
                TN.matmul(banks[bkr][0:1, 0:NE], onesf_s[:],
                          res_fn(ct)[:, tc * NE:(tc + 1) * NE],
                          start=(ct == 0), stop=(ct == DT - 1))
            tm = TN.mark(TEN)
            V.wait(TEN, tm)
            V.tensor_scalar_mul(rowC[0:1, tc * NE:(tc + 1) * NE],
                                banks[bkr][0:1, 0:NE], 1.0 / D)
            m_rsum = V.mark(VEC)
            bank_done(bkr, VEC, m_rsum)
        V.wait(VEC, VEC.n)
        V.tensor_mul(rowB[0:1, 0:TK], rowC[0:1, 0:TK], rowA[0:1, 0:TK])
        V.mark(VEC)
        V.wait(VEC, VEC.n)
        V.tensor_scalar_mul(rowB[0:1, 0:TK], rowB[0:1, 0:TK], -1.0)
        mrB = V.mark(VEC)
        for tt in range(8):
            bkS = take_bank()
            for ct in range(DT):
                TN.matmul(banks[bkS][:, 0:E],
                          res_fn(ct)[:, tt * 128:(tt + 1) * 128],
                          rTf_s[:, ct * E:(ct + 1) * E],
                          start=(ct == 0), stop=(ct == DT - 1))
            tmS = TN.mark(TEN)
            bkO = take_bank()
            TN.wait(VEC, mrB)
            TN.matmul(banks[bkO][:, 0:E], rowB[0:1, tt * 128:(tt + 1) * 128],
                      hs_s[0:1, 8:16], start=True, stop=True)
            tmO = TN.mark(TEN)
            bkT = take_bank()
            TN.transpose(banks[bkT][:, 0:1], rowA[0:1, tt * 128:(tt + 1) * 128],
                         idf32_s[0:1, 0:1])
            tmT = TN.mark(TEN)
            V.wait(TEN, tmT)
            V.tensor_copy(col_s[:], banks[bkT][:, 0:1])
            bank_done(bkT, VEC, V.mark(VEC))
            V.wait(VEC, VEC.n)
            V.tensor_scalar_mul(sc8_s[:, tt * E:(tt + 1) * E],
                                banks[bkS][:, 0:E], col_s[:, 0:1])
            bank_done(bkS, VEC, V.mark(VEC))
            V.wait(VEC, VEC.n)
            V.tensor_add(sc8_s[:, tt * E:(tt + 1) * E],
                         sc8_s[:, tt * E:(tt + 1) * E], banks[bkO][:, 0:E])
            bank_done(bkO, VEC, V.mark(VEC))
            V.wait(VEC, VEC.n)

        def S3(t):
            return t[:].rearrange("p (tt e) -> p tt e", e=E)

        m1 = sc8b_s[:, 0:8]
        negm1 = sc8b_s[:, 8:16]
        m2 = sc8b_s[:, 16:24]
        dtmp = sc8b_s[:, 24:32]
        rcp = sc8b_s[:, 32:40]

        def vs():
            V.mark(VEC)
            V.wait(VEC, VEC.n)

        def ss():
            SC.mark(SCA)
            SC.wait(SCA, SCA.n)

        V.wait(VEC, VEC.n)
        V.reduce_max(m1.rearrange("p (a b) -> p a b", b=1), S3(sc8_s), axis=AX.X)
        vs()
        V.tensor_scalar_mul(negm1, m1, -1.0)
        vs()
        for tt in range(8):
            V.tensor_scalar(top2_s[:, tt * E:(tt + 1) * E],
                            sc8_s[:, tt * E:(tt + 1) * E],
                            m1[:, tt:tt + 1], None, ALU.is_equal)
            vs()
        V.tensor_scalar_mul(top2_s[:], top2_s[:], -2.0e30)
        vs()
        V.tensor_add(top2_s[:], top2_s[:], sc8_s[:])
        vs()
        V.reduce_max(m2.rearrange("p (a b) -> p a b", b=1), S3(top2_s), axis=AX.X)
        vs()
        for tt in range(8):
            SC.wait(VEC, VEC.n)
            SC.emit("activation", top2_s[:, tt * E:(tt + 1) * E],
                    sc8_s[:, tt * E:(tt + 1) * E], AF.Exp,
                    bias=negm1[:, tt:tt + 1])
            ss()
        V.wait(SCA, SCA.n)
        V.tensor_add(dtmp, m2, negm1)
        vs()
        SC.wait(VEC, VEC.n)
        SC.emit("activation", dtmp, dtmp, AF.Exp)
        ss()
        V.wait(SCA, SCA.n)
        V.tensor_scalar_add(dtmp, dtmp, 1.0)
        vs()
        V.reciprocal(rcp, dtmp)
        vs()
        for tt in range(8):
            V.tensor_scalar(top2b_s[:, tt * E:(tt + 1) * E],
                            sc8_s[:, tt * E:(tt + 1) * E],
                            m2[:, tt:tt + 1], None, ALU.is_ge)
            vs()
        V.tensor_mul(top2b_s[:], top2b_s[:], top2_s[:])
        vs()
        for tt in range(8):
            V.tensor_scalar_mul(wfin_s[:, tt * E:(tt + 1) * E],
                                top2b_s[:, tt * E:(tt + 1) * E],
                                rcp[:, tt:tt + 1])
            vs()
        w_done = VEC.n

        if upto == "moescores":
            G.wait(VEC, VEC.n)
            G.wait(SCA, SCA.n)
            G.dma_start(out=yout_d.ap()[0:128, 0:64], in_=sc8_s[:])
            G.mark(STO, 16)
            G.dma_start(out=yout_d.ap()[128:256, 0:64], in_=wfin_s[:])
            G.mark(STO, 16)
            G.dma_start(out=yout_d.ap()[256:384, 0:64], in_=top2_s[:])
            G.mark(STO, 16)
            G.dma_start(out=yout_d.ap()[384:512, 0:8], in_=m1)
            G.mark(STO, 16)
            G.dma_start(out=yout_d.ap()[512:640, 0:8], in_=m2)
            G.mark(STO, 16)
            return "STOP"

        for tt in range(8):
            TN.wait(VEC, w_done if tt == 0 else VEC.n)
            TN.transpose(pT8[0:8, 0:128], wfin_s[:, tt * E:(tt + 1) * E],
                         idf_s[:])
            tm = TN.mark(TEN)
            V.wait(TEN, tm)
            V.tensor_copy(wTs_s[0:8, tt * 128:(tt + 1) * 128], pT8[0:8, 0:128])
            V.mark(VEC)
        wts_done = VEC.n

        silu_marks = {}
        for e in range(E):
            for tc in range(2):
                bk = take_bank()
                TN.wait(VEC, wts_done)
                TN.matmul(banks[bk][:, 0:NE], sel8_s[:, e * 128:(e + 1) * 128],
                          wTs_s[0:8, tc * NE:(tc + 1) * NE],
                          start=True, stop=True)
                tm = TN.mark(TEN)
                V.wait(TEN, tm)
                V.tensor_copy(wb_s[:, tc * NE:(tc + 1) * NE], banks[bk][:, 0:NE])
                bank_done(bk, VEC, V.mark(VEC))
            wb_done = VEC.n
            for wc in range(2):
                wi, ldwn = wload(w1_d, wc * 256, 256, rows=D, expert=e)
                TN.wait(*ldwn)
                for fo in range(2):
                    ft = wc * 2 + fo
                    for ts in range(2):
                        bk = take_bank()
                        for ct in range(DT):
                            TN.matmul(banks[bk][:, 0:NE],
                                      WB[wi][:, ct * 256 + fo * 128:
                                             ct * 256 + fo * 128 + 128],
                                      xn_t(ct)[:, ts * NE:(ts + 1) * NE],
                                      start=(ct == 0), stop=(ct == DT - 1))
                        tm = TN.mark(TEN)
                        slot = e * 4096 + ft * 1024 + ts * NE
                        SC.wait(TEN, tm)
                        SC.emit("activation", xs_b(slot, NE), banks[bk][:, 0:NE],
                                AF.Silu)
                        ms = SC.mark(SCA)
                        silu_marks[(e, ft, ts)] = ms
                        bank_done(bk, SCA, ms)
                wb_used(wi)
            for wc in range(2):
                wi, ldwn = wload(w3_d, wc * 256, 256, rows=D, expert=e)
                TN.wait(*ldwn)
                for fo in range(2):
                    ft = wc * 2 + fo
                    for ts in range(2):
                        bk = take_bank()
                        for ct in range(DT):
                            TN.matmul(banks[bk][:, 0:NE],
                                      WB[wi][:, ct * 256 + fo * 128:
                                             ct * 256 + fo * 128 + 128],
                                      xn_t(ct)[:, ts * NE:(ts + 1) * NE],
                                      start=(ct == 0), stop=(ct == DT - 1))
                        tm = TN.mark(TEN)
                        slot = e * 4096 + ft * 1024 + ts * NE
                        V.wait(TEN, tm)
                        V.wait(SCA, silu_marks[(e, ft, ts)])
                        V.wait(VEC, wb_done)
                        V.tensor_mul(xs_b(slot, NE), banks[bk][:, 0:NE],
                                     xs_b(slot, NE))
                        bank_done(bk, VEC, V.mark(VEC))
                        V.tensor_mul(xs_b(slot, NE), xs_b(slot, NE),
                                     wb_s[:, ts * NE:(ts + 1) * NE])
                        V.mark(VEC)
                wb_used(wi)
        h_done = VEC.n

        for dg in range(8):
            acc = {}
            for e in range(E):
                wi, ldwn = wload(w2_d, dg * 256, 256, rows=FF, expert=e)
                TN.wait(*ldwn)
                TN.wait(VEC, h_done)
                for do_ in range(2):
                    for ts in range(2):
                        if e == 0:
                            acc[(do_, ts)] = take_bank()
                        bk = acc[(do_, ts)]
                        for ftile in range(4):
                            TN.matmul(banks[bk][:, 0:NE],
                                      WB[wi][:, ftile * 256 + do_ * 128:
                                             ftile * 256 + do_ * 128 + 128],
                                      xs_b(e * 4096 + ftile * 1024 + ts * NE,
                                           NE),
                                      start=(e == 0 and ftile == 0),
                                      stop=(e == E - 1 and ftile == 3))
                        if e == E - 1:
                            tm = TN.mark(TEN)
                            dt = dg * 2 + do_
                            V.wait(TEN, tm)
                            V.tensor_add(res_fn(dt)[:, ts * NE:(ts + 1) * NE],
                                         res_fn(dt)[:, ts * NE:(ts + 1) * NE],
                                         banks[bk][:, 0:NE])
                            bank_done(bk, VEC, V.mark(VEC))
                TN.mark(TEN)
                wb_used(wi)


    QB = 0
    KB = DT * TK
    AVB = DT * TK

    def RESb(col, L):
        return RES[:].bitcast(BF16)[:, col:col + L]

    def emit_attention():
        G.wait(VEC, VEC.n)
        G.wait(SCA, SCA.n)
        G.reg_alu(rg["roff"], rg["rmyhalf"], D * TK, ALU.mult)
        G.dma_start(out=bass.AP(x2sh_d, rg["roff"],
                                [[TK, 128], [128 * TK, DT], [1, TK]]),
                    in_=RES3()[:, :, 1:1 + TK])
        G.mark(GPD, 16)
        pair_signal(GPD.n)
        e1_mark = pair_poll()

        slot_fence = {0: (TEN, 0), 1: (TEN, 0), 2: (TEN, 0), 3: (TEN, 0)}

        def ln3_xn(th):
            if th == 0:
                G.wait(TEN, TEN.n)
                G.wait(VEC, VEC.n)
                G.wait(SCA, SCA.n)
            G.wait(sems["SPD"], e1_mark)
            ch = ((0, NE), (NE, NE))
            stat_banks = [(take_bank(), take_bank()) for _ in ch]
            for bk2 in stat_banks:
                bank_hold.add(bk2[0]); bank_hold.add(bk2[1])
            sqf = {0: (SCA, 0), 1: (SCA, 0)}
            for dt in range(DT):
                for tc in range(2):
                    sl = (dt * 2 + tc) % 2
                    sem, cnt = slot_fence[sl]
                    G.wait(sem, cnt)
                    G.dma_start(out=scr_b(PR_SLOT[sl], NE),
                                in_=x2sh_d.ap()[th, dt * 128:(dt + 1) * 128,
                                                tc * NE:(tc + 1) * NE])
                    G.mark(XSEM[sl], 16)
                    gm = (XSEM[sl], XSEM[sl].n)
                    sem, cnt = sqf[sl]
                    SC.wait(sem, cnt)
                    SC.wait(*gm)
                    SC.emit("activation", scr_b(PR_SLOT[2 + sl], NE),
                            scr_b(PR_SLOT[sl], NE), AF.Square)
                    ms = SC.mark(SCA)
                    b1, b2 = stat_banks[tc]
                    TN.wait(*gm)
                    TN.wait(SCA, ms)
                    TN.matmul(banks[b1][0:1, 0:NE], onesb_s[:],
                              scr_b(PR_SLOT[sl], NE),
                              start=(dt == 0), stop=(dt == DT - 1))
                    TN.matmul(banks[b2][0:1, 0:NE], onesb_s[:],
                              scr_b(PR_SLOT[2 + sl], NE),
                              start=(dt == 0), stop=(dt == DT - 1))
                    tm = TN.mark(TEN)
                    slot_fence[sl] = (TEN, tm)
                    slot_fence[2 + sl] = (TEN, tm)
                    sqf[sl] = (TEN, tm)
            stat_done = TEN.n
            for bk2 in stat_banks:
                bank_hold.discard(bk2[0]); bank_hold.discard(bk2[1])
            out_chunks = [emit_rows(b1, b2, c0, Lc, stat_done)
                          for (b1, b2), (c0, Lc) in zip(stat_banks, ch)]
            for (c0, Lc, bA, bB, mT) in out_chunks:
                tc = 0 if c0 == 0 else 1
                for dt in range(DT):
                    sl = (dt * 2 + tc) % 4
                    sem, cnt = slot_fence[sl]
                    G.wait(sem, cnt)
                    G.dma_start(out=scr_b(PR_SLOT[sl], NE),
                                in_=x2sh_d.ap()[th, dt * 128:(dt + 1) * 128,
                                                c0:c0 + Lc])
                    G.mark(XSEM[sl], 16)
                    V.wait(XSEM[sl], XSEM[sl].n)
                    V.wait(TEN, mT)
                    t = xs_f(TS32, Lc)
                    V.tensor_mul(t, scr_b(PR_SLOT[sl], Lc), banks[bA][:, 0:Lc])
                    V.tensor_add(HB[:, dt * TK + c0: dt * TK + c0 + Lc], t,
                                 banks[bB][:, 0:Lc])
                    vm = V.mark(VEC)
                    slot_fence[sl] = (VEC, vm)
                bank_done(bA, VEC, VEC.n)
                bank_done(bB, VEC, VEC.n)
            return VEC.n

        def qkv(th, xn_mark):
            for wdram, outbase in ((qTh_d, QB), (kTh_d, KB)):
                for wc in range(4):
                    wi, ldwn = wload(wdram, wc * 256, 256, rows=D)
                    TN.wait(*ldwn)
                    TN.wait(VEC, xn_mark)
                    for eo in range(2):
                        hl = wc * 2 + eo
                        for tc in range(2):
                            bk = take_bank()
                            for ct in range(DT):
                                TN.matmul(banks[bk][:, 0:NE],
                                          wtile(wi, ct, eo * 128, 128, 256),
                                          HB[:, ct * TK + tc * NE:
                                             ct * TK + (tc + 1) * NE],
                                          start=(ct == 0), stop=(ct == DT - 1))
                            tm = TN.mark(TEN)
                            SC.wait(TEN, tm)
                            SC.emit("activation",
                                    RESb(outbase + hl * T + th * TK + tc * NE,
                                         NE),
                                    banks[bk][:, 0:NE], AF.Copy)
                            bank_done(bk, SCA, SC.mark(SCA))
                    wb_used(wi)
            for wc in range(4):
                wi, ldwn = wload(vTh_d, wc * 256, 256, rows=D)
                TN.wait(*ldwn)
                TN.wait(VEC, xn_mark)
                for tt in range(8):
                    bk = take_bank()
                    for ct in range(DT):
                        TN.matmul(banks[bk][:, 0:256],
                                  HB[:, ct * TK + tt * 128:
                                     ct * TK + (tt + 1) * 128],
                                  WB[wi][:, ct * 256:(ct + 1) * 256],
                                  start=(ct == 0), stop=(ct == DT - 1))
                    tm = TN.mark(TEN)
                    SC.wait(TEN, tm)
                    SC.emit("activation",
                            xs_b((th * 8 + tt) * TK + wc * 256, 256),
                            banks[bk][:, 0:256], AF.Copy)
                    bank_done(bk, SCA, SC.mark(SCA))
                wb_used(wi)

        for th in (0, 1):
            xm = ln3_xn(th)
            qkv(th, xm)
        qkv_done_sc = SCA.n

        TN.wait(SCA, qkv_done_sc)
        TN.wait(*ldw_pre)
        for hl in range(8):
            for qc in range(4):
                bk = take_bank()
                TN.matmul(banks[bk][:, 0:NE], gT_s[:],
                          RESb(QB + hl * T + qc * NE, NE), start=True,
                          stop=True)
                tm = TN.mark(TEN)
                SC.wait(TEN, tm)
                SC.emit("activation", HB[:, hl * T + qc * NE:
                                         hl * T + (qc + 1) * NE],
                        banks[bk][:, 0:NE], AF.Sigmoid, bias=gb_s[:, 0:1])
                bank_done(bk, SCA, SC.mark(SCA))
        gates_sc = SCA.n

        psf = {i: (TEN, 0) for i in range(4)}
        pslot_rr = [0]

        def head_bound(hl):
            for src_base, hc in ((QB, 8), (KB, 12)):
                for qc in range(4):
                    sl = pslot_rr[0] % 4
                    pslot_rr[0] += 1
                    sem, cnt = psf[sl]
                    SC.wait(sem, cnt)
                    SC.emit("activation", scr_b(PR_SLOT[sl], NE),
                            RESb(src_base + hl * T + qc * NE, NE), AF.Square)
                    ms = SC.mark(SCA)
                    bkr = take_bank()
                    TN.wait(SCA, ms)
                    TN.matmul(banks[bkr][0:1, 0:NE], onesb_s[:],
                              scr_b(PR_SLOT[sl], NE), start=True, stop=True)
                    tm = TN.mark(TEN)
                    psf[sl] = (TEN, tm)
                    V.wait(TEN, tm)
                    V.reduce_max(hs_s[0:1, hc + qc:hc + qc + 1],
                                 banks[bkr][0:1, 0:NE], axis=AX.X)
                    vm = V.mark(VEC)
                    V.wait(VEC, vm)
                    bank_done(bkr, VEC, vm)
            V.reduce_max(hs_s[0:1, 0:1], hs_s[0:1, 8:12], axis=AX.X)
            V.mark(VEC)
            V.wait(VEC, VEC.n)
            V.reduce_max(hs_s[0:1, 1:2], hs_s[0:1, 12:16], axis=AX.X)
            V.mark(VEC)
            V.wait(VEC, VEC.n)
            SC.wait(VEC, VEC.n)
            SC.emit("activation", hs_s[0:1, 2:4], hs_s[0:1, 0:2], AF.Ln)
            SC.mark(SCA)
            SC.wait(SCA, SCA.n)
            V.wait(SCA, SCA.n)
            V.tensor_add(hs_s[0:1, 4:5], hs_s[0:1, 2:3], hs_s[0:1, 3:4])
            V.mark(VEC)
            V.wait(VEC, VEC.n)
            SC.wait(VEC, VEC.n)
            SC.emit("activation", hs_s[0:1, 5:6], hs_s[0:1, 4:5], AF.Exp,
                    scale=0.5)
            SC.mark(SCA)
            SC.wait(SCA, SCA.n)
            V.wait(SCA, SCA.n)
            V.tensor_scalar(hs_s[0:1, 6:7], hs_s[0:1, 5:6], SCALE, 1.0,
                            ALU.mult, ALU.add)
            V.mark(VEC)
            V.wait(VEC, VEC.n)
            V.tensor_scalar_mul(hs_s[0:1, 6:7], hs_s[0:1, 6:7], -1.0)
            mb = V.mark(VEC)
            V.wait(VEC, mb)
            bk = take_bank()
            TN.wait(VEC, mb)
            TN.matmul(banks[bk][:, 0:1], ones1f_s[:], hs_s[0:1, 6:7],
                      start=True, stop=True)
            tm = TN.mark(TEN)
            V.wait(TEN, tm)
            V.tensor_copy(mh_s[:, hl:hl + 1], banks[bk][:, 0:1])
            vm = V.mark(VEC)
            V.wait(VEC, vm)
            bank_done(bk, VEC, vm)
            return vm

        rrow_free = {0: (TEN, 0), 1: (TEN, 0)}
        pending = [None]

        def jphase(hl, qc, idx):
            av_bk = take_bank()
            bank_hold.add(av_bk)
            rs_bk = take_bank()
            bank_hold.add(rs_bk)
            jmax = 4 * (qc + 1)
            pend = []

            def emit_scores(j):
                rel = max(0, 128 * j - NE * qc)
                L = NE - rel
                sbk = take_bank()
                TN.matmul(banks[sbk][:, 0:L],
                          RESb(KB + hl * T + j * 128, 128),
                          RESb(QB + hl * T + qc * NE + rel, L),
                          start=True, stop=True)
                tms = TN.mark(TEN)
                sl = pslot_rr[0] % 4
                pslot_rr[0] += 1
                sem, cnt = psf[sl]
                SC.wait(sem, cnt)
                SC.wait(TEN, tms)
                SC.wait(VEC, mh_ready[0])
                SC.emit("activation", scr_b(PR_SLOT[sl], L),
                        banks[sbk][:, 0:L], AF.Exp,
                        bias=mh_s[:, hl:hl + 1], scale=SCALE)
                msp = SC.mark(SCA)
                bank_done(sbk, SCA, msp)
                fence = (SCA, msp)
                if j >= 4 * qc:
                    V.wait(SCA, msp)
                    dcol = 128 * j - NE * qc - rel
                    V.tensor_mul(scr_b(PR_SLOT[sl] + dcol, 128),
                                 scr_b(PR_SLOT[sl] + dcol, 128), tri_s[:])
                    fence = (VEC, V.mark(VEC))
                return (j, rel, L, sl, fence)

            def emit_av(st2):
                j, rel, L, sl, fence = st2
                TN.wait(*fence)
                TN.matmul(banks[av_bk][:, rel:rel + L],
                          xs_b(j * TK + hl * 128, 128),
                          scr_b(PR_SLOT[sl], L),
                          start=(j == 0), stop=(j == jmax - 1))
                TN.matmul(banks[rs_bk][0:1, rel:rel + L], onesb_s[:],
                          scr_b(PR_SLOT[sl], L),
                          start=(j == 0), stop=(j == jmax - 1))
                tm = TN.mark(TEN)
                psf[sl] = (TEN, tm)

            for j in range(jmax):
                pend.append(emit_scores(j))
                if len(pend) >= 2:
                    emit_av(pend.pop(0))
            while pend:
                emit_av(pend.pop(0))
            return {"hl": hl, "qc": qc, "av": av_bk, "rs": rs_bk,
                    "av_tm": TEN.n, "ri": idx % 2}

        def normA(st):
            rrow = rowA if st["ri"] == 0 else rowC
            V.wait(TEN, st["av_tm"])
            V.wait(*rrow_free[st["ri"]])
            V.tensor_copy(rrow[0:1, 0:NE], banks[st["rs"]][0:1, 0:NE])
            vm = V.mark(VEC)
            V.wait(VEC, vm)
            bank_hold.discard(st["rs"])
            bank_done(st["rs"], VEC, vm)
            SC.wait(VEC, vm)
            SC.emit("activation", rowB[0:1, 0:NE], rrow[0:1, 0:NE], AF.Ln)
            SC.mark(SCA)
            SC.wait(SCA, SCA.n)
            SC.emit("activation", rrow[0:1, 0:NE], rowB[0:1, 0:NE], AF.Exp,
                    scale=-1.0)
            st["msr"] = SC.mark(SCA)

        def normB(st):
            hl, qc = st["hl"], st["qc"]
            rrow = rowA if st["ri"] == 0 else rowC
            rb_bk = take_bank()
            TN.wait(SCA, st["msr"])
            TN.matmul(banks[rb_bk][:, 0:NE], ones1f_s[:], rrow[0:1, 0:NE],
                      start=True, stop=True)
            tmr = TN.mark(TEN)
            rrow_free[st["ri"]] = (TEN, tmr)
            V.wait(TEN, tmr)
            V.tensor_copy(rb_s[:, 0:NE], banks[rb_bk][:, 0:NE])
            vm = V.mark(VEC)
            V.wait(VEC, vm)
            bank_done(rb_bk, VEC, vm)
            avc = AVB + hl * T + qc * NE
            V.tensor_mul(xs_b(avc, NE), banks[st["av"]][:, 0:NE],
                         rb_s[:, 0:NE])
            bank_hold.discard(st["av"])
            bank_done(st["av"], VEC, V.mark(VEC))
            V.wait(SCA, gates_sc)
            V.tensor_mul(xs_b(avc, NE), xs_b(avc, NE),
                         HB[:, hl * T + qc * NE: hl * T + (qc + 1) * NE])
            V.mark(VEC)

        idx = 0
        mh_ready = [0]
        for hl in range(8):
            mh_ready[0] = head_bound(hl)
            for qc in range(4):
                st = jphase(hl, qc, idx)
                normA(st)
                if pending[0] is not None:
                    normB(pending[0])
                pending[0] = st
                idx += 1
        normB(pending[0])
        att_done_v = VEC.n

        G.wait(VEC, att_done_v)
        G.wait(SCA, SCA.n)
        for th in (0, 1):
            G.reg_alu(rg["roff"], rg["rmyhalf"], 8 * 128 * TK, ALU.mult)
            G.reg_alu(rg["roff"], rg["roff"], th * 2 * 8 * 128 * TK, ALU.add)
            G.dma_start(out=bass.AP(avsh_d, rg["roff"],
                                    [[TK, 128], [128 * TK, 8], [1, TK]]),
                        in_=XS[:, AVB:AVB + 8 * T]
                        .rearrange("p (hl t) -> p hl t", t=T)
                        [:, :, th * TK:(th + 1) * TK])
            G.mark(GPD, 16)
        pair_signal(GPD.n)
        e2_mark = pair_poll()
        G.wait(sems["SPD"], e2_mark)
        G.wait(TEN, TEN.n)
        G.reg_alu(rg["roff"], rg["rmyhalf"], 2 * 8 * 128 * TK, ALU.mult)
        G.dma_start(out=XS[:, 0:DT * TK]
                    .rearrange("p (ct t) -> p ct t", t=TK),
                    in_=bass.AP(avsh_d, rg["roff"],
                                [[TK, 128], [128 * TK, DT], [1, TK]]))
        avall_ready = G.mark(GPD, 16)
        G.wait(GPD, avall_ready)

        for tc in range(2):
            G.wait(GPD, GPD.n)
            if tc == 1:
                G.wait(VEC, VEC.n)
            G.reg_alu(rg["roff"], rg["rmyhalf"], D * TK, ALU.mult)
            G.reg_alu(rg["roff2"], rg["roff"], tc * NE, ALU.add)
            G.dma_start(out=XS[:].bitcast(F32)
                        [:, DT * NE:DT * NE + DT * NE]
                        .rearrange("p (dt l) -> p dt l", dt=DT),
                        in_=bass.AP(x2sh_d, rg["roff2"],
                                    [[TK, 128], [128 * TK, DT], [1, NE]]))
            x2m = G.mark(GPD, 16)
            for wc in range(8):
                wi, ldwn = wload(wT_d["oT"], wc * 256, 256, rows=D)
                TN.wait(*ldwn)
                TN.wait(GPD, avall_ready)
                for eo in range(2):
                    et = wc * 2 + eo
                    bk = take_bank()
                    for ct in range(DT):
                        TN.matmul(banks[bk][:, 0:NE],
                                  wtile(wi, ct, eo * 128, 128, 256),
                                  xs_b(ct * TK + tc * NE, NE),
                                  start=(ct == 0), stop=(ct == DT - 1))
                    tm = TN.mark(TEN)
                    V.wait(TEN, tm)
                    V.wait(GPD, x2m)
                    V.tensor_add(RES[:, et * TK + tc * NE:
                                     et * TK + (tc + 1) * NE],
                                 banks[bk][:, 0:NE],
                                 XS[:].bitcast(F32)
                                 [:, DT * NE + et * NE:
                                  DT * NE + (et + 1) * NE])
                    bank_done(bk, VEC, V.mark(VEC))
                wb_used(wi)

    V.wait(LDA, ld_pre)
    if upto == "pre":
        emit_output(1, ST1); return _finalize(nc, ctx, (S, G, TN, V, SC))

    def vsync_small(Lc):
        if Lc < 64:
            V.mark(VEC)
            V.wait(VEC, VEC.n)

    def ln0_src(dt):
        return xs_f(dt * TK, TK)

    def ln0_apply(dt, c0, Lc, bkA, bkB):
        V.tensor_mul(res_t(dt, 1 + c0, Lc), ln0_src(dt)[:, c0:c0 + Lc],
                     bkA[:, 0:Lc])
        vsync_small(Lc)
        V.tensor_add(res_t(dt, 1 + c0, Lc), res_t(dt, 1 + c0, Lc),
                     bkB[:, 0:Lc])
        vsync_small(Lc)

    emit_ln(ln0_src, TK, lambda dt: hb_t(dt, 1, TK), ln0_apply)
    if upto == "ln0":
        emit_output(1, ST1); return _finalize(nc, ctx, (S, G, TN, V, SC))

    def ln1_apply(dt, c0, Lc, bkA, bkB):
        t = xs_f(TS32, Lc)
        V.tensor_mul(t, res_t(dt, 1 + c0, Lc), bkA[:, 0:Lc])
        vsync_small(Lc)
        V.tensor_add(t, t, bkB[:, 0:Lc])
        vsync_small(Lc)
        V.tensor_scalar(hb_t(dt, 1 + c0, Lc), t, vec_s["ln1g"][:, dt:dt + 1],
                        vec_s["ln1b"][:, dt:dt + 1], ALU.mult, ALU.add)
        vsync_small(Lc)

    emit_ln(lambda dt: res_t(dt, 1, TK), TK, lambda dt: hb_t(dt, 1, TK),
            ln1_apply)
    if upto == "ln1":
        emit_output(1, ST1); return _finalize(nc, ctx, (S, G, TN, V, SC))

    def col_exchange(sb_col, dram_h, into_view, mask):
        G.wait(VEC, VEC.n)
        G.reg_alu(rg["roff"], rg["rmyhalf"], 128 * DT, ALU.mult)
        G.dma_start(out=bass.AP(dram_h, rg["roff"], [[DT, 128], [1, DT]]),
                    in_=sb_col[:])
        G.mark(GPD, 16)
        pair_signal(GPD.n)
        pair_poll()
        S.reg_alu(sg["roff"], sg["rsibhalf"], 128 * DT, ALU.mult)
        S.dma_start(out=sb_col[:],
                    in_=bass.AP(dram_h, sg["roff"], [[DT, 128], [1, DT]]))
        S.mark(sems["SPD"], 16)
        V.wait(sems["SPD"], sems["SPD"].n)
        if mask:
            V.tensor_scalar_mul(sb_col[:], sb_col[:], maskh_s[:, 0:1])
            V.mark(VEC)
            V.wait(VEC, VEC.n)
        V.tensor_copy(into_view, sb_col[:].rearrange("p (dt o) -> p dt o", o=1))
        m = V.mark(VEC)
        V.wait(VEC, m)
        return m

    pre_k = {0: wload(wT_d["wkT"], 0, WCH), WCH: wload(wT_d["wkT"], WCH, WCH)}
    V.tensor_copy(colbuf_s[:].rearrange("p (dt o) -> p dt o", o=1),
                  HB3()[:, :, TK:TK + 1])
    V.mark(VEC)
    hcol_done = col_exchange(colbuf_s, colsh_d, HB3()[:, :, 0:1], True)
    if upto == "e0a":
        emit_output(1, ST1); return _finalize(nc, ctx, (S, G, TN, V, SC))

    XA, XB = 0, DT * TK
    TCH = ((0, NE), (NE, NE))

    def lerp(mu_name, dst0, wait_ten=None, hbase=0):
        if wait_ten is not None:
            V.wait(TEN, wait_ten)
        V.wait(SCA, SCA.n)
        marks = {}
        for (c0, L) in TCH:
            for dt in range(DT):
                V.tensor_scalar_mul(xs_b(dst0 + dt * TK + c0, L),
                                    hb_t(dt, hbase + 1 + c0, L),
                                    vec_s[mu_name][:, dt:dt + 1])
                V.scalar_tensor_tensor(xs_b(dst0 + dt * TK + c0, L),
                                       hb_t(dt, hbase + c0, L),
                                       omu_s[mu_name][:, dt:dt + 1],
                                       xs_b(dst0 + dt * TK + c0, L),
                                       ALU.mult, ALU.add)
            marks[c0] = V.mark(VEC)
        return marks

    def matpass(wdram, rhs_fn, evac, rt=DT, e_total=DT, rhs_ready=None,
                pre=None):
        for ecol0 in range(0, e_total * 128, WCH):
            if pre and ecol0 in pre:
                wi, ldwn = pre[ecol0]
            else:
                wi, ldwn = wload(wdram, ecol0, WCH, rows=rt * 128)
            TN.wait(*ldwn)
            for eo in range(WCH // 128):
                et = ecol0 // 128 + eo
                for (c0, L) in TCH:
                    if rhs_ready is not None:
                        TN.wait(VEC, rhs_ready[c0] if isinstance(rhs_ready, dict)
                                else rhs_ready[1])
                    bk = take_bank()
                    for ct in range(rt):
                        TN.matmul(banks[bk][:, 0:L],
                                  wtile(wi, ct, eo * 128, 128),
                                  rhs_fn(ct, c0, L),
                                  start=(ct == 0), stop=(ct == rt - 1))
                    tm = TN.mark(TEN)
                    evac(bk, et, c0, L, tm)
            wb_used(wi)

    def xa_rhs(ct, c0, L):
        return xs_b(XA + ct * TK + c0, L)

    def xb_rhs(ct, c0, L):
        return xs_b(XB + ct * TK + c0, L)

    mlp = lerp("muk", XA, wait_ten=None)

    def evac_k(bk, et, c0, L, tm):
        SC.wait(TEN, tm)
        SC.emit("activation", xs_b(XB + et * TK + c0, L), banks[bk][:, 0:L],
                AF.Sigmoid, bias=ubias2_s[:, et:et + 1])
        bank_done(bk, SCA, SC.mark(SCA))

    matpass(wT_d["wkT"], xa_rhs, evac_k, rhs_ready=mlp, pre=pre_k)
    kpass_ten = TEN.n

    mlp = lerp("mur", XA, wait_ten=kpass_ten)
    slot_state = {0: 0, 1: 0, 2: 0, 3: 0}
    slot_rr = [0]

    def evac_r(bk, et, c0, L, tm):
        sl = slot_rr[0] % 4
        slot_rr[0] += 1
        SC.wait(TEN, tm)
        SC.wait(VEC, slot_state[sl])
        SC.emit("activation", scr_b(PR_SLOT[sl], L), banks[bk][:, 0:L],
                AF.Sigmoid)
        ms = SC.mark(SCA)
        bank_done(bk, SCA, ms)
        V.wait(SCA, ms)
        V.tensor_mul(xs_b(XB + et * TK + c0, L), scr_b(PR_SLOT[sl], L),
                     xs_b(XB + et * TK + c0, L))
        slot_state[sl] = V.mark(VEC)

    matpass(wT_d["wrT"], xa_rhs, evac_r, rhs_ready=mlp)
    rpass_ten = TEN.n

    mlp = lerp("muv", XA, wait_ten=rpass_ten)

    def evac_v(bk, et, c0, L, tm):
        V.wait(TEN, tm)
        V.tensor_mul(xs_b(XB + et * TK + c0, L), banks[bk][:, 0:L],
                     xs_b(XB + et * TK + c0, L))
        bank_done(bk, VEC, V.mark(VEC))

    matpass(wT_d["wvT"], xa_rhs, evac_v, rhs_ready=mlp)
    vpass_vec = VEC.n

    def evac_o(bk, et, c0, L, tm):
        V.wait(TEN, tm)
        V.tensor_add(res_t(et, 1 + c0, L), res_t(et, 1 + c0, L),
                     banks[bk][:, 0:L])
        bank_done(bk, VEC, V.mark(VEC))

    matpass(wT_d["woT"], xb_rhs, evac_o, rhs_ready={0: vpass_vec, NE: vpass_vec})
    if upto == "tm":
        emit_output(1, ST1); return _finalize(nc, ctx, (S, G, TN, V, SC))

    V.wait(VEC, VEC.n)
    V.tensor_copy(colbuf2_s[:].rearrange("p (dt o) -> p dt o", o=1),
                  RES3()[:, :, TK:TK + 1])
    V.mark(VEC)
    col_exchange2_done = col_exchange(colbuf2_s, colxm_d, RES3()[:, :, 0:1],
                                      False)

    def ln2_apply(dt, c0, Lc, bkA, bkB):
        t = xs_f(TS32, Lc)
        V.tensor_mul(t, res_t(dt, c0, Lc), bkA[:, 0:Lc])
        vsync_small(Lc)
        V.tensor_add(t, t, bkB[:, 0:Lc])
        vsync_small(Lc)
        V.tensor_scalar(hb_t(dt, c0, Lc), t, vec_s["ln2g"][:, dt:dt + 1],
                        vec_s["ln2b"][:, dt:dt + 1], ALU.mult, ALU.add)
        vsync_small(Lc)

    V.wait(VEC, col_exchange2_done)
    emit_ln(lambda dt: res_t(dt, 0, ST1), ST1, lambda dt: hb_t(dt, 0, ST1),
            ln2_apply, debug_tap=2)
    V.tensor_scalar_mul(HB3()[:, :, 0:1], HB3()[:, :, 0:1], maskh_s[:, 0:1])
    h2_done = V.mark(VEC)
    if upto == "ln2rows":
        G.wait(VEC, VEC.n)
        G.wait(SCA, SCA.n)
        G.dma_start(out=yout_d.ap()[0:1, 0:TK], in_=rowA[0:1, 0:TK])
        G.mark(STO, 16)
        G.dma_start(out=yout_d.ap()[1:2, 0:TK], in_=rowB[0:1, 0:TK])
        G.mark(STO, 16)
        G.dma_start(out=yout_d.ap()[2:3, 0:2], in_=rowA[0:1, 1023:1025])
        G.mark(STO, 16)
        G.dma_start(out=yout_d.ap()[3:4, 0:2], in_=rowB[0:1, 1023:1025])
        G.mark(STO, 16)
        G.dma_start(out=yout_d.ap()[4:5, 0:4], in_=hs_s[0:1, 0:4])
        G.mark(STO, 16)
        G.wait(STO, STO.n)
        S.wait(STO, STO.n)
        return _finalize(nc, ctx, (S, G, TN, V, SC))
    if upto == "h2":
        G.wait(VEC, VEC.n)
        G.wait(SCA, SCA.n)
        G.dma_start(out=yout_d.ap().rearrange("(dt p) l -> p dt l", p=128),
                    in_=HB3()[:, :, 1:1 + TK])
        G.mark(STO, 16)
        G.wait(STO, STO.n)
        S.wait(STO, STO.n)
        return _finalize(nc, ctx, (S, G, TN, V, SC))

    V.wait(VEC, h2_done)
    lerp("cmuk", XA)
    m_lerp2 = lerp("cmur", XB)

    def evac_k2(bk, et, c0, L, tm):
        sl = slot_rr[0] % 4
        slot_rr[0] += 1
        SC.wait(TEN, tm)
        SC.wait(VEC, slot_state[sl])
        SC.emit("activation", scr_b(PR_SLOT[sl], L), banks[bk][:, 0:L],
                AF.Relu)
        mr = SC.mark(SCA)
        SC.wait(SCA, mr)
        SC.emit("activation", hb_t(et, 1 + c0, L), scr_b(PR_SLOT[sl], L),
                AF.Square)
        ms = SC.mark(SCA)
        slot_state[sl] = ms
        bank_done(bk, SCA, ms)

    matpass(wT_d["cwkT"], xa_rhs, evac_k2, rhs_ready=m_lerp2)

    def evac_r2(bk, et, c0, L, tm):
        SC.wait(TEN, tm)
        SC.emit("activation", xs_b(XA + et * TK + c0, L), banks[bk][:, 0:L],
                AF.Sigmoid)
        bank_done(bk, SCA, SC.mark(SCA))

    matpass(wT_d["cwrT"], xb_rhs, evac_r2)

    def k2_rhs(ct, c0, L):
        return hb_t(ct, 1 + c0, L)

    def evac_cm(bk, et, c0, L, tm):
        t = xs_f(TS32, L)
        V.wait(TEN, tm)
        V.tensor_mul(t, banks[bk][:, 0:L], xs_b(XA + et * TK + c0, L))
        bank_done(bk, VEC, V.mark(VEC))
        V.tensor_add(res_t(et, 1 + c0, L), res_t(et, 1 + c0, L), t)
        V.mark(VEC)

    matpass(wT_d["cwvT"], k2_rhs, evac_cm)
    cm_ten = TEN.n

    for (c0, L) in TCH:
        S.wait(TEN, cm_ten)
        S.wait(VEC, VEC.n)
        dst = XS[:].bitcast(F32)[:, DT * NE: DT * NE + DT * L] \
            .rearrange("p (dt l) -> p dt l", dt=DT)
        S.dma_start(out=dst,
                    in_=x0T_d.ap()[:, c0:c0 + L]
                    .rearrange("(dt p) l -> p dt l", p=128))
        mx = S.mark(LDA, 16)
        V.wait(LDA, mx)
        for dt in range(DT):
            V.tensor_add(res_t(dt, 1 + c0, L), res_t(dt, 1 + c0, L),
                         XS[:].bitcast(F32)[:, DT * NE + dt * L:
                                            DT * NE + dt * L + L])
        V.mark(VEC)
    s1_done = V.mark(VEC)

    if upto == "s1":
        emit_output(1, ST1)
        return _finalize(nc, ctx, (S, G, TN, V, SC))

    r = emit_moe(lambda dt: res_t(dt, 1, TK))
    if r == "STOP":
        G.wait(STO, STO.n)
        S.wait(STO, STO.n)
        return _finalize(nc, ctx, (S, G, TN, V, SC))
    if upto == "s2":
        emit_output(1, ST1)
        return _finalize(nc, ctx, (S, G, TN, V, SC))

    emit_attention()
    if upto == "s3":
        emit_output(0, TK)
        return _finalize(nc, ctx, (S, G, TN, V, SC))

    emit_moe(lambda dt: RES[:, dt * TK: (dt + 1) * TK])
    emit_output(0, TK)
    return _finalize(nc, ctx, (S, G, TN, V, SC))


def _finalize(nc, ctx, streams):
    S, G, TN, V, SC = streams
    with nc.Block() as block:
        def runner(st):
            def run(eng):
                guards = []
                for r in st.q:
                    if isinstance(r, tuple):
                        if r[0] == "WHILE_BEGIN":
                            g = eng.While(r[1])
                            g.__enter__()
                            guards.append(g)
                        else:
                            guards.pop().__exit__(None, None, None)
                        continue
                    inst = getattr(eng, r.op)(*r.args, **r.kwargs)
                    for sem, k in r.incs:
                        inst.then_inc(sem.h if isinstance(sem, Sem) else sem, k)
            return run
        block.sync(runner(S))
        block.gpsimd(runner(G))
        block.tensor(runner(TN))
        block.vector(runner(V))
        block.scalar(runner(SC))
    ctx.close()
    return nc


def stage_inputs(inputs, upto="full"):
    f = np.float32
    x = np.asarray(inputs["L0_attn_a_x"], dtype=f)

    def t2(a):
        return np.ascontiguousarray(np.asarray(a, dtype=f).T)

    def vec128(a):
        return np.ascontiguousarray(np.asarray(a, dtype=f).reshape(DT, 128).T)

    wTs = {
        "wrT": t2(inputs["L0_attn_a_tm_Wr"]),
        "wkT": t2(inputs["L0_attn_a_tm_Wk"]),
        "wvT": t2(inputs["L0_attn_a_tm_Wv"]),
        "woT": t2(inputs["L0_attn_a_tm_Wout"]),
        "cwrT": t2(inputs["L0_attn_a_cm_Wr"]),
        "cwkT": t2(inputs["L0_attn_a_cm_Wk"]),
        "cwvT": t2(inputs["L0_attn_a_cm_Wv"]),
        "oT": t2(inputs["L0_attn_b_o_w"]),
    }
    gT = t2(inputs["L0_attn_b_gate_w"])
    gb = np.ascontiguousarray(
        np.asarray(inputs["L0_attn_b_gate_b"], dtype=f).reshape(HD, 1))
    rT = np.ascontiguousarray(
        np.asarray(inputs["L0_ffn_router_w"], dtype=f).T
        .reshape(DT, 128, E).transpose(1, 0, 2).reshape(128, DT * E))
    w1 = np.ascontiguousarray(np.asarray(inputs["L0_ffn_w1"], dtype=f))
    w3 = np.ascontiguousarray(np.asarray(inputs["L0_ffn_w3"], dtype=f))
    w2 = np.ascontiguousarray(np.asarray(inputs["L0_ffn_w2"], dtype=f))
    vecs = {
        "ln1g": vec128(inputs["L0_attn_a_ln1_g"]),
        "ln1b": vec128(inputs["L0_attn_a_ln1_b"]),
        "ln2g": vec128(inputs["L0_attn_a_ln2_g"]),
        "ln2b": vec128(inputs["L0_attn_a_ln2_b"]),
        "mur": vec128(inputs["L0_attn_a_tm_mu_r"]),
        "muk": vec128(inputs["L0_attn_a_tm_mu_k"]),
        "muv": vec128(inputs["L0_attn_a_tm_mu_v"]),
        "cmur": vec128(inputs["L0_attn_a_cm_mu_r"]),
        "cmuk": vec128(inputs["L0_attn_a_cm_mu_k"]),
        "ubias": vec128(inputs["L0_attn_a_tm_u"]),
    }
    idf = np.eye(128, dtype=ml_dtypes.bfloat16)
    idf32 = np.eye(128, dtype=f)
    sel8 = np.zeros((E, E * 128), dtype=ml_dtypes.bfloat16)
    for e in range(E):
        sel8[e, e * 128:(e + 1) * 128] = 1.0
    tri = np.tril(np.ones((128, 128), dtype=f)).T
    tri = tri.astype(ml_dtypes.bfloat16)
    rng = np.random.default_rng()
    nonce = int(rng.integers(1, 2**31 - 16))
    flagv = np.arange(1, 9, dtype=np.uint32) + np.uint32(nonce)
    flagv = flagv.reshape(1, 8)

    in_maps = []
    for c in range(8):
        bb, half = c // 2, c % 2
        xt = np.ascontiguousarray(x[bb, half * TK:(half + 1) * TK, :].T)
        qfull = np.asarray(inputs["L0_attn_b_q_w"], dtype=f).T
        kfull = np.asarray(inputs["L0_attn_b_k_w"], dtype=f).T
        vfull = np.asarray(inputs["L0_attn_b_v_w"], dtype=f).T
        sl = slice(half * 1024, (half + 1) * 1024)
        qTh = np.ascontiguousarray(qfull[:, sl])
        kTh = np.ascontiguousarray(kfull[:, sl])
        vTh = np.ascontiguousarray(vfull[:, sl])
        m = {
            "x0T": xt, "qTh": qTh, "kTh": kTh, "vTh": vTh,
            "gT": gT, "gb": gb, "rT": rT,
            "w1": w1, "w3": w3, "w2": w2,
            "maskh": np.full((128, 1), float(half), dtype=f),
            "idf": idf, "idf32": idf32, "sel8": sel8, "tri": tri,
            "flagv": flagv,
            "hinfo": np.array([[half, 1 - half, 0, 0]], dtype=np.uint32),
        }
        m.update(wTs)
        m.update(vecs)
        in_maps.append(m)
    return in_maps


_NC_CACHE = {}


def kernel(_upto=None, **inputs):
    if _upto is None:
        _upto = os.environ.get("K_UPTO", "full")
    upto = _upto
    in_maps = stage_inputs(inputs, upto)
    if upto not in _NC_CACHE:
        _NC_CACHE[upto] = build_program(upto)
    nc = _NC_CACHE[upto]
    res = run_bass_kernel_spmd(nc, in_maps, core_ids=list(range(8)))
    out = np.zeros((B, T, D), dtype=np.float32)
    for c in range(8):
        bb, half = c // 2, c % 2
        out[bb, half * TK:(half + 1) * TK, :] = res.results[c]["yout"].T
    return out


if __name__ == "__main__":
    pass
